# version 16
# baseline (speedup 1.0000x reference)
import contextlib
import numpy as np
import concourse.bass as bass
import concourse.mybir as mybir
from concourse.bass_utils import run_bass_kernel_spmd

F32 = mybir.dt.float32
BF16 = mybir.dt.bfloat16
I8 = mybir.dt.int8
I32 = mybir.dt.int32
AF = mybir.ActivationFunctionType
ALU = mybir.AluOpType
AX = mybir.AxisListType

C = 1024
NB = 8
H = 4
LC = 128
CL_NEAR = -8.0
TH_FAR = 0.4
WN = ["wq", "wk", "wv", "wg", "wf", "wo"]

DBG_TAPS = []

ENGS = ("sync", "tensor", "vector", "scalar", "gpsimd")
OWNER = {"pe": "tensor", "dve": "vector", "act": "scalar", "pool": "gpsimd"}


class Prog:
    def __init__(self, nc, es):
        self.nc = nc
        self.q = {e: [] for e in ENGS}
        names = ("pe", "dve", "act", "pool", "lx0", "lx1", "lw0", "lw1", "lwi", "lv", "st0", "st1")
        self.sem = {n: es.enter_context(nc.semaphore("s_" + n)) for n in names}
        self.cnt = {n: 0 for n in names}
        self.wmax = {}
        self.waited = {}

    def do(self, eng, fn, sig=None):
        amt, name = 0, None
        force = False
        if sig is not None:
            name, amt = (sig, 1) if isinstance(sig, str) else sig
            if name.endswith("!"):
                force, name = True, name[:-1]
            self.cnt[name] += amt
        if force or name == "pool":
            self.q[eng].append(("g", fn, name, amt, -1))
        else:
            self.q[eng].append(("x", fn, name, amt))
        return (name, self.cnt[name]) if name else None

    def wait(self, eng, *evs):
        for ev in evs:
            if ev is None:
                continue
            name, v = ev
            if OWNER.get(name) == eng:
                continue
            key = (eng, name)
            if self.wmax.get(key, 0) >= v:
                continue
            self.wmax[key] = v
            self.waited.setdefault(name, set()).add(v)
            self.q[eng].append(("w", name, v, 0))

    def emit(self, block):
        for ename in ENGS:
            deco = getattr(block, ename)
            ops = self.q[ename]

            def body(eng, _ops=ops):
                for op in _ops:
                    kind = op[0]
                    if kind == "w":
                        eng.wait_ge(self.sem[op[1]], op[2])
                    elif kind == "g":
                        _, fn, name, amt, cum = op
                        fn(eng)
                        eng.maybe_drain_then_inc((self.sem[name], amt), fusable=False)
                    else:
                        _, fn, name, amt = op
                        inst = fn(eng)
                        if name is not None:
                            assert inst is not None
                            inst.then_inc(self.sem[name], amt)
            deco(body)


def build_program(nch):
    T = nch * LC
    nc = bass.Bass("TRN2", target_bir_lowering=False, dynamic_dma_scratch_size=2048)
    x_d = nc.dram_tensor("x", [T, C], F32, kind="ExternalInput").ap()
    w_d = {w: nc.dram_tensor(w, [C, C], I32 if w == "wf" else F32,
                             kind="ExternalInput").ap() for w in WN}
    vec_d = nc.dram_tensor("vecs", [8, C], F32, kind="ExternalInput").ap()
    y_d = nc.dram_tensor("y", [T, C], F32, kind="ExternalOutput").ap()

    es = contextlib.ExitStack()
    with es:
        def sbt(name, shape, dt=F32):
            return es.enter_context(nc.sbuf_tensor(name, shape, dt))

        wbf = sbt("wbf", [128, 6 * NB * C], BF16)
        Pt = sbt("Pt", [128, 6 * C])
        BIASB = sbt("BIASB", [128, C])
        MM4 = sbt("MM4", [128, 3 * 512])
        idf = sbt("idf", [128, 128])
        idb = sbt("idb", [128, 128], BF16)
        cvec = sbt("cvec", [128, 48])
        cdec = sbt("cdec", [128, 48])
        cols = sbt("cols", [128, 192])
        st_f = sbt("st_f", [128, 8 * 257])
        st_b = sbt("st_b", [128, 8 * 257], BF16)
        scr_st = sbt("scr_st", [128, 4 * 257])
        x_tm = [sbt("x_tm0", [128, C]), sbt("x_tm1", [128, C])]
        xn_tm = sbt("xn_tm", [128, C])
        xn_fm = sbt("xn_fm", [128, C])
        mixq = sbt("mixq", [128, C], BF16)
        mixk = sbt("mixk", [128, C], BF16)
        mixv = sbt("mixv", [128, C], BF16)
        mixg = sbt("mixg", [128, C], BF16)
        scrB = sbt("scrB", [128, C])
        qn_tm = sbt("qn_tm", [128, C])
        kn_tm = sbt("kn_tm", [128, C])
        kexp_fm = sbt("kexp_fm", [128, C])
        att_tm = sbt("att_tm", [128, C])
        qt_b = sbt("qt_b", [128, C], BF16)
        qn_b = sbt("qn_b", [128, C], BF16)
        kn_b = sbt("kn_b", [128, C], BF16)
        ka_b = sbt("ka_b", [128, C], BF16)
        qf_b = sbt("qf_b", [128, C], BF16)
        kf_b = sbt("kf_b", [128, C], BF16)
        xq_tmb = sbt("xq_tmb", [128, C], BF16)
        kl_b = sbt("kl_b", [128, C], BF16)
        sc_b = sbt("sc_b", [128, 512], BF16)
        xq_fm = sbt("xq_fm", [128, C], BF16)
        v_pl = sbt("v_pl", [128, 4 * 257], BF16)
        xi8 = sc_b[:].bitcast(I8)
        out_sb = [sbt("out0", [128, C]), sbt("out1", [128, C])]
        prevl = sbt("prevl", [128, 8])
        pb = [es.enter_context(nc.psum_tensor(f"pb{i}", [128, 512], F32)) for i in range(7)]
        pbf = es.enter_context(nc.psum_tensor("pb7", [128, 1024], BF16))

        vrow = kexp_fm

        P = Prog(nc, es)
        do, wt = P.do, P.wait
        ev = {}
        dbg_outs = {}

        def tap(tag, c, buf, evt, eng="scalar"):
            if (tag, c) not in DBG_TAPS:
                return
            name = f"dbg_{tag}_{c}"
            shp = list(buf.shape)
            dt_ = buf.dtype
            od = nc.dram_tensor(name, shp, dt_, kind="ExternalOutput").ap()
            dbg_outs[name] = od
            wt(eng, evt)
            do(eng, lambda e, od=od, buf=buf: e.dma_start(out=od, in_=buf), ("st0", 16))

        def col(t, i):
            return t[:, i:i + 1]

        CN = ("sx", "sq", "m", "var", "rstd", "nmr", "rf", "amax", "tq",
              "qmul", "r_t", "qs0", "qs1", "rq", "ks0", "ks1", "rk",
              "rz0", "rz1", "rz2", "rz3", "t0", "t1", "t2")

        def cc(name, c):
            return col(cols, 4 * CN.index(name) + 96 * (c & 1))

        ev_ldx = [None] * (nch + 2)
        ev_ldx[0] = do("sync", lambda e: e.dma_start(out=x_tm[0][:], in_=x_d[0:128, :]), ("lx0", 16))
        if nch > 1:
            ev_ldx[1] = do("sync", lambda e: e.dma_start(out=x_tm[1][:], in_=x_d[128:256, :]), ("lx1", 16))

        ev_ms0 = do("gpsimd", lambda e: (e.memset(prevl[:], 0.0), e.memset(v_pl[:], 1.0),
                                         e.memset(cols[:], -333.25),
                                         e.memset(att_tm[0:1, 512:640], 1.0))[-1], "pool")

        ev_cast = {}
        ev_stg_free = [None, None]
        cast_eng = {"wq": "scalar", "wk": "scalar", "wv": "vector", "wg": "vector",
                    "wf": "gpsimd", "wo": "scalar"}
        cast_sem = {"scalar": "act", "vector": "dve", "gpsimd": "pool"}
        stg = [st_f[:, 0:1024], st_f[:, 1024:2048]]
        for wi, w in enumerate(WN):
            eng = cast_eng[w]
            for j in range(NB):
                src = w_d[w][128 * j:128 * (j + 1), :]
                dst = wbf[:, (wi * NB + j) * C:(wi * NB + j + 1) * C]
                ldq = "sync"
                if w == "wf":
                    sp = j & 1
                    wt(ldq, ev_stg_free[sp])
                    ev_ld = do(ldq, lambda e, s=src, sp=sp: e.dma_start(
                        out=stg[sp].bitcast(I32), in_=s),
                        ("lw0" if sp == 0 else "lw1", 16))
                    wt(eng, ev_ld)
                    ev_c = do(eng, lambda e, d=dst, sp=sp: e.tensor_copy(
                        d, stg[sp].bitcast(I32)), cast_sem[eng])
                    ev_stg_free[sp] = ev_c
                else:
                    sp = j & 1
                    wt(ldq, ev_stg_free[sp])
                    ev_ld = do(ldq, lambda e, s=src, sp=sp: e.dma_start(out=stg[sp], in_=s),
                               ("lw0" if sp == 0 else "lw1", 16))
                    wt(eng, ev_ld)
                    if eng == "scalar":
                        sc = 0.25 if w == "wo" else 1.0
                        ev_c = do(eng, lambda e, d=dst, sp=sp, sc=sc:
                                  e.activation(d, stg[sp], AF.Copy, scale=sc), "act")
                    else:
                        ev_c = do(eng, lambda e, d=dst, sp=sp:
                                  e.tensor_copy(d, stg[sp]), cast_sem[eng])
                    ev_stg_free[sp] = ev_c
            ev_cast[w] = ev_c

        def iota_f(e):
            e.iota(att_tm[:, 0:128], pattern=[[1, 128]], base=0,
                   channel_multiplier=0, allow_small_or_imprecise_dtypes=True)
            e.iota(att_tm[:, 128:256], pattern=[[0, 8], [1, 16]], base=0,
                   channel_multiplier=0, allow_small_or_imprecise_dtypes=True)
            e.iota(att_tm[:, 256:384], pattern=[[1, 8], [0, 16]], base=0,
                   channel_multiplier=0, allow_small_or_imprecise_dtypes=True)
            e.iota(col(cdec, 32), pattern=[[0, 1]], base=0,
                   channel_multiplier=1, allow_small_or_imprecise_dtypes=True)
            e.iota(col(cdec, 33), pattern=[[0, 1]], base=127,
                   channel_multiplier=-1, allow_small_or_imprecise_dtypes=True)
            e.iota(kn_tm[:, 0:512], pattern=[[0, 4], [1, 128]], base=0,
                   channel_multiplier=0, allow_small_or_imprecise_dtypes=True)
            return e.iota(kn_tm[:, 512:1024], pattern=[[0, 4], [1, 8], [0, 16]], base=0,
                          channel_multiplier=0, allow_small_or_imprecise_dtypes=True)
        ev_iota = do("gpsimd", iota_f, "pool")

        wt("vector", ev_iota)
        ev_id = do("vector", lambda e: e.tensor_scalar(idf[:], att_tm[:, 0:128],
                                                       col(cdec, 32), None, ALU.is_equal), "dve")
        ev_idb = do("vector", lambda e: e.tensor_copy(idb[:], idf[:]), "dve")

        wt("tensor", ev_id)
        ev_bs = do("tensor", lambda e: e.transpose(pb[6][:, 0:1], att_tm[0:1, 256:384],
                                                   idf[0:1, 0:1]), "pe")
        wt("vector", ev_bs)
        ev_bscp = do("vector", lambda e: e.tensor_copy(col(cdec, 34), pb[6][:, 0:1]), "dve!")

        def mk_masks(e):
            e.tensor_scalar(qn_tm[:, 0:512], kn_tm[:, 0:512], col(cdec, 32), None, ALU.subtract)
            e.tensor_scalar(kn_tm[:, 512:1024], kn_tm[:, 512:1024], col(cdec, 34), None,
                            ALU.subtract)
            e.drain()
            e.tensor_scalar(MM4[:, 0:512], kn_tm[:, 512:1024], 0.0, 1e30, ALU.is_equal, ALU.mult)
            e.tensor_scalar(qn_tm[:, 0:512], qn_tm[:, 0:512], -0.5, None, ALU.is_ge)
            e.drain()
            e.tensor_tensor(MM4[:, 0:512], MM4[:, 0:512], qn_tm[:, 0:512], ALU.mult)
            e.tensor_scalar(MM4[:, 512:1024], kn_tm[:, 512:1024], 1.0, 1e30, ALU.is_equal,
                            ALU.mult)
            return e.tensor_scalar(MM4[:, 1024:1536], kn_tm[:, 512:1024], 1.5, 1e30,
                                   ALU.is_ge, ALU.mult)
        ev_mm4 = do("vector", mk_masks, "dve")

        vstg = kexp_fm[0:1, :]
        ev_vtr = ev_bscp
        for vi in range(6):
            wt("sync", ev_vtr)
            ev_ldvi = do("sync", lambda e, vi=vi: e.dma_start(out=vstg, in_=vec_d[vi:vi + 1, :]),
                         ("lv", 16))
            wt("tensor", ev_ldvi, ev_bscp)

            def tr_f(e, vi=vi):
                last = None
                for j in range(NB):
                    last = e.transpose(pb[6][:, vi * 8 + j: vi * 8 + j + 1],
                                       vstg[0:1, 128 * j:128 * (j + 1)], idf[0:1, 0:1])
                return last
            ev_vtr = do("tensor", tr_f, "pe")
        wt("vector", ev_vtr)
        ev_cvec = do("vector", lambda e: e.tensor_copy(cvec[:], pb[6][:, 0:48]), "dve!")

        wt("scalar", ev_cvec)
        ev_d1s = do("scalar", lambda e: e.activation(cdec[:, 0:8], cvec[:, 0:8], AF.Exp), "act!")
        wt("vector", ev_d1s)

        def dec2(e):
            e.tensor_scalar(cdec[:, 0:8], cdec[:, 0:8], -1.0, None, ALU.mult)
            e.drain()
            e.tensor_scalar(cdec[:, 8:16], cdec[:, 0:8], CL_NEAR, None, ALU.max)
            return e.tensor_scalar(cdec[:, 16:24], cdec[:, 0:8], float(np.log(TH_FAR)),
                                   None, ALU.max)
        ev_d2 = do("vector", dec2, "dve!")
        wt("scalar", ev_d2)
        ev_wlc = do("scalar", lambda e: (
            e.activation(cdec[:, 24:32], cdec[:, 0:8], AF.Exp, scale=float(LC)),
            e.activation(cdec[:, 36:44], cdec[:, 8:16], AF.Exp, scale=16.0))[-1], "act!")

        RSPEC = [(0, 0, 1.0, 1.0, 0),
                 (1, 1, 1.0, -7.5, 8),
                 (2, 1, -1.0, 7.5, 8),
                 (3, 0, 1.0, -63.5, 16),
                 (4, 0, -1.0, 63.5, 16),
                 (5, 0, -1.0, 127.0, 0)]
        ev_p = None
        for (pi, rsel, ma, ab, lnb) in RSPEC:
            rsrc = att_tm[:, 0:128] if rsel == 0 else att_tm[:, 128:256]
            dst0 = pi * C

            def ramp_f(e, rsrc=rsrc, ma=ma, ab=ab):
                return e.tensor_scalar(att_tm[:, 384:512], rsrc, ma, ab, ALU.mult, ALU.add)
            do("vector", ramp_f, "dve")

            def blocks_f(e, dst0=dst0, lnb=lnb):
                last = None
                for j in range(NB):
                    last = e.tensor_scalar(Pt[:, dst0 + 128 * j: dst0 + 128 * (j + 1)],
                                           att_tm[:, 384:512], col(cdec, lnb + j), None, ALU.mult)
                return last
            ev_b = do("vector", blocks_f, "dve")
            wt("scalar", ev_b)
            ev_p = do("scalar", lambda e, dst0=dst0: e.activation(
                Pt[:, dst0:dst0 + C], Pt[:, dst0:dst0 + C], AF.Exp), "act")
        ev_pt = ev_p

        wt("sync", ev_vtr)
        ev_ldb = do("sync", lambda e: e.dma_start(out=vstg, in_=vec_d[6:7, :]), ("lv", 16))
        ev_bc_done = None
        ev_bc = None
        for half in range(2):
            sl = slice(512 * half, 512 * (half + 1))
            wt("tensor", ev_ms0, ev_ldb, ev_bc_done)

            def bc_f(e, sl=sl):
                return e.matmul(pb[2][:, 0:512], att_tm[0:1, 512:640], vstg[0:1, sl],
                                start=True, stop=True)
            ev_bc = do("tensor", bc_f, "pe")
            wt("vector", ev_bc, ev_mm4)

            def bc_ev(e, sl=sl):
                return e.tensor_copy(BIASB[:, sl], pb[2][:, 0:512])
            ev_bc_done = do("vector", bc_ev, "dve")
        wt("sync", ev_bc)
        ev_ldw = do("sync", lambda e: e.dma_start(out=vstg[0:1, 0:1], in_=vec_d[7:8, 0:1]),
                    ("lv", 16))
        wt("tensor", ev_ldw)
        ev_bcw = do("tensor", lambda e: e.matmul(pb[3][:, 0:1], att_tm[0:1, 512:640],
                                                 vstg[0:1, 0:1], start=True, stop=True), "pe")
        wt("vector", ev_bcw)
        ev_bc_done = do("vector", lambda e: e.tensor_copy(col(cdec, 35), pb[3][:, 0:1]), "dve!")
        ev["setup_v"] = ev_bc_done
        ev["setup_a"] = ev_pt

        wt("gpsimd", ev_cast["wq"], ev_cast["wk"], ev_cast["wv"], ev_cast["wg"],
           ev_cast["wf"], ev_cast["wo"])
        ev_ms1 = do("gpsimd", lambda e: (e.memset(st_f[:], 0.0),
                                         e.memset(st_b[:], 0.0))[-1], "pool")

        def tail(c, first=False):
            p = c & 1
            xx = out_sb[p]
            wt("vector", ev_ldx[c])
            ev[f"sx{c}"] = do("vector", lambda e: e.reduce_sum(cc("sx", c), x_tm[p][:],
                                                              axis=AX.X), "dve!")
            wt("scalar", ev_ldx[c], ev.get(f"store_{c-2}"))
            if first:
                wt("scalar", ev["setup_v"])
            ev[f"sq{c}"] = do("scalar", lambda e: e.activation(xx[:], x_tm[p][:], AF.Square,
                                                              accum_out=cc("sq", c)), "act!")
            wt("vector", ev[f"sq{c}"])

            def ln_tiny(e):
                e.tensor_scalar(cc("m", c), cc("sx", c), 1.0 / C, None, ALU.mult)
                e.tensor_scalar(cc("t1", c), cc("sx", c), cc("sx", c), 1.0 / (C * C),
                                ALU.mult, ALU.mult)
                e.tensor_scalar(cc("t2", c), cc("sq", c), 1.0 / C, 1e-5, ALU.mult, ALU.add)
                e.drain()
                return e.tensor_tensor(cc("t0", c), cc("t2", c), cc("t1", c), ALU.subtract)
            ev[f"var{c}"] = do("vector", ln_tiny, "dve!")
            wt("scalar", ev[f"var{c}"])

            def rstd_f(e):
                e.activation(cc("t1", c), cc("t0", c), AF.Ln)
                e.drain()
                return e.activation(cc("rstd", c), cc("t1", c), AF.Exp, scale=-0.5)
            ev[f"rstd{c}"] = do("scalar", rstd_f, "act!")
            wt("vector", ev[f"rstd{c}"])

            def nmr_f(e):
                e.tensor_scalar(cc("nmr", c), cc("m", c), cc("rstd", c), -1.0,
                                ALU.mult, ALU.mult)
                e.reciprocal(cc("t1", c), cc("t0", c))
                e.drain()
                return e.tensor_scalar(cc("t2", c), cc("t1", c), -1e-5, 1.000001,
                                       ALU.mult, ALU.add)
            ev[f"nmr{c}"] = do("vector", nmr_f, "dve!")
            wt("scalar", ev[f"nmr{c}"], ev.get(f"amax{c-1}"))

            def xn_f(e):
                e.activation(cc("t1", c), cc("t2", c), AF.Ln)
                e.drain()
                return e.activation(cc("rf", c), cc("t1", c), AF.Exp, scale=-0.5)
            ev[f"xn{c}"] = do("scalar", xn_f, "act!")
            wt("vector", ev[f"xn{c}"])
            wt("vector", ev.get(f"xi8_{c-1}"))
            ev[f"xn{c}"] = do("vector", lambda e: e.tensor_scalar(
                xn_tm[:], x_tm[p][:], cc("rstd", c), cc("nmr", c), ALU.mult, ALU.add), "dve")
            tap("xn", c, xn_tm[:], ev[f"xn{c}"])
            wt("vector", ev[f"xn{c}"])
            ev[f"amax{c}"] = do("vector", lambda e: e.reduce_max(
                cc("amax", c), xn_tm[:], axis=AX.X, apply_absolute_value=True), "dve!")

            def qscal_f(e):
                e.tensor_scalar(cc("tq", c), cc("amax", c), cc("rf", c), 1e-5,
                                ALU.mult, ALU.max)
                e.drain()
                e.tensor_scalar(cc("r_t", c), cc("tq", c), col(cdec, 35), 1.0 / 127.0,
                                ALU.mult, ALU.mult)
                e.reciprocal(cc("t0", c), cc("tq", c))
                e.drain()
                return e.tensor_scalar(cc("qmul", c), cc("t0", c), cc("rf", c), 127.0,
                                       ALU.mult, ALU.mult)
            ev[f"qscal{c}"] = do("vector", qscal_f, "dve!")

            if first:
                wt("tensor", ev[f"xn{c}"], ev_cvec, ev["setup_v"], ev_idb)
            else:
                wt("tensor", ev[f"xn{c}"], ev[f"attstt_h3_{c-1}"], ev[f"scmin_{c-1}"])

            def txn_f(e):
                last = None
                for j in range(NB):
                    dpb = pb[6] if j < 4 else pb[2]
                    last = e.transpose(dpb[:, 128 * (j % 4):128 * (j % 4 + 1)],
                                       xn_tm[:, 128 * j:128 * (j + 1)], idf[:])
                return last
            ev[f"txn_pe_{c}"] = do("tensor", txn_f, "pe")
            wt("scalar", ev[f"txn_pe_{c}"], ev.get(f"mix_v_{c-1}"), ev.get(f"mix_g_{c-1}"))

            def txn_ev(e):
                e.activation(xn_fm[:, 0:512], pb[6][:, 0:512], AF.Copy)
                return e.activation(xn_fm[:, 512:1024], pb[2][:, 0:512], AF.Copy)
            ev[f"txn_ev_{c}"] = do("scalar", txn_ev, "act")

            tap("xnfm", c, xn_fm[:], ev[f"txn_ev_{c}"])
            wt("vector", ev[f"txn_ev_{c}"], ev.get(f"mix_v_{c-1}"),
               ev.get(f"attstt_h3_{c-1}"))

            def xx_f(e):
                for j in range(NB):
                    e.tensor_copy(xx[:, 128 * j + 1:128 * j + 128],
                                  xn_fm[:, 128 * j:128 * j + 127])
                h3b = xx[:].rearrange("p (j t) -> p j t", j=NB)
                return e.tensor_copy(h3b[:, :, 0:1], prevl[:].rearrange("p (j o) -> p j o", o=1))
            ev[f"xx{c}"] = do("vector", xx_f, "dve")

            def pl_f(e):
                x3 = xn_fm[:].rearrange("p (j t) -> p j t", j=NB)
                return e.tensor_copy(prevl[:].rearrange("p (j o) -> p j o", o=1),
                                     x3[:, :, 127:128])
            ev[f"prevl{c}"] = do("vector", pl_f, "dve")

            wt("vector", ev.get(f"g_pe1_{c-1}"))
            ev[f"d1_{c}"] = do("vector", lambda e: e.tensor_tensor(scrB[:], xn_fm[:], xx[:],
                                                                  ALU.subtract), "dve")
            MQ, MK, MV, MG = 16, 24, 32, 40
            mix_out = [(mixq, MQ, "q"), (mixk, MK, "k"), (mixv, MV, "v"), (mixg, MG, "g")]

            def mk_mix(dst, base, j):
                def f(e):
                    sl = slice(128 * j, 128 * (j + 1))
                    return e.scalar_tensor_tensor(dst[:, sl], scrB[:, sl], col(cvec, base + j),
                                                  xx[:, sl], ALU.mult, ALU.add)
                return f
            lastv = None
            for dst, base, tag_ in mix_out:
                for j in range(NB):
                    lastv = do("vector", mk_mix(dst, base, j), "dve")
                ev[f"mix_{tag_}_{c}"] = lastv
            tap("colsa", c, cols[:], ev[f"qscal{c}"])
            tap("wgw", c, wbf[:, 24 * C:32 * C], ev.get(f"g_pe1_{c-1}"))
            tap("wvw", c, wbf[:, 16 * C:24 * C], ev.get(f"g_pe1_{c-1}"))
            tap("cvec", c, cvec[:], ev[f"qscal{c}"])
            tap("xx", c, xx[:], lastv)
            tap("d1", c, scrB[:], lastv)
            tap("mixq", c, mixq[:], lastv)
            tap("mixk", c, mixk[:], lastv)
            tap("mixv", c, mixv[:], lastv)
            tap("mixg", c, mixg[:], lastv)

        def mm_proj(e, psum_half, src_bf, wi, half):
            last = None
            for j in range(NB):
                last = e.matmul(psum_half,
                                src_bf[:, 128 * j:128 * (j + 1)],
                                wbf[:, (wi * NB + j) * C + 512 * half:
                                    (wi * NB + j) * C + 512 * (half + 1)],
                                start=(j == 0), stop=(j == NB - 1))
            return last

        def chunk_body(c):
            p = c & 1
            wt("tensor", ev[f"mix_q_{c}"], ev_cast["wq"], ev.get(f"stv_h2_{c-1}"))
            for half in range(2):
                ev[f"q_pe{half}_{c}"] = do("tensor", lambda e, h=half: mm_proj(
                    e, pb[h][:, 0:512], mixq, 0, h), "pe")
            wt("tensor", ev[f"mix_k_{c}"], ev_cast["wk"], ev.get(f"tatt_ev_{c-1}"))
            for half in range(2):
                ev[f"k_pe{half}_{c}"] = do("tensor", lambda e, h=half: mm_proj(
                    e, pb[4 + h][:, 0:512], mixk, 1, h), "pe")

            wt("scalar", ev[f"q_pe1_{c}"], ev.get(f"gf_{c-1}"))
            ev[f"qsq{c}"] = do("scalar", lambda e: (
                e.activation(out_sb[p][:, 0:512], pb[0][:, 0:512], AF.Square,
                             accum_out=cc("qs0", c)),
                e.activation(out_sb[p][:, 512:1024], pb[1][:, 0:512], AF.Square,
                             accum_out=cc("qs1", c)))[-1], "act!")
            wt("vector", ev[f"qsq{c}"])
            ev[f"rq1{c}"] = do("vector", lambda e: e.tensor_tensor(
                cc("t0", c), cc("qs0", c), cc("qs1", c), ALU.add), "dve!")
            wt("scalar", ev[f"rq1{c}"], ev.get(f"gf_{c-1}"))
            ev[f"rq2{c}"] = do("scalar", lambda e: (
                e.activation(cc("t1", c), cc("t0", c), AF.Ln, scale=1.0 / C),
                e.drain(),
                e.activation(cc("rq", c), cc("t1", c), AF.Exp, scale=-0.5),
                e.drain())[-2], "act")
            if c == 0:
                wt("vector", ev_mm4)
            wt("vector", ev[f"rq2{c}"], ev[f"q_pe1_{c}"], ev.get(f"gf_{c-1}"))
            ev[f"qn_ev_{c}"] = do("vector", lambda e: (
                e.tensor_scalar(qn_tm[:, 0:512], pb[0][:, 0:512], cc("rq", c), None,
                                ALU.mult),
                e.tensor_scalar(qn_tm[:, 512:1024], pb[1][:, 0:512], cc("rq", c), None,
                                ALU.mult))[-1], "dve")

            wt("scalar", ev[f"k_pe1_{c}"])
            ev[f"ksq{c}"] = do("scalar", lambda e: (
                e.activation(out_sb[p][:, 0:512], pb[4][:, 0:512], AF.Square,
                             accum_out=cc("ks0", c)),
                e.activation(out_sb[p][:, 512:1024], pb[5][:, 0:512], AF.Square,
                             accum_out=cc("ks1", c)))[-1], "act!")
            wt("vector", ev[f"ksq{c}"])
            ev[f"rk1{c}"] = do("vector", lambda e: e.tensor_tensor(
                cc("t2", c), cc("ks0", c), cc("ks1", c), ALU.add), "dve!")
            wt("scalar", ev[f"rk1{c}"], ev.get(f"gf_{c-1}"))
            ev[f"rk2{c}"] = do("scalar", lambda e: (
                e.activation(cc("t1", c), cc("t2", c), AF.Ln, scale=1.0 / C),
                e.drain(),
                e.activation(cc("rk", c), cc("t1", c), AF.Exp, scale=-0.5),
                e.drain())[-2], "act")
            wt("vector", ev[f"rk2{c}"], ev[f"k_pe1_{c}"])
            ev[f"kn_ev_{c}"] = do("vector", lambda e: (
                e.tensor_scalar(kn_tm[:, 0:512], pb[4][:, 0:512], cc("rk", c), None,
                                ALU.mult),
                e.tensor_scalar(kn_tm[:, 512:1024], pb[5][:, 0:512], cc("rk", c), None,
                                ALU.mult))[-1], "dve")

            wt("scalar", ev[f"qscal{c}"], ev.get(f"xqb_{c-1}"), ev.get(f"S_pe_h3_{c-1}"))
            ev[f"xi8_{c}"] = do("scalar", lambda e: e.activation(
                xi8, xn_tm[:], AF.Copy, scale=cc("qmul", c)), "act")

            tap("qn", c, qn_tm[:], ev[f"qn_ev_{c}"])
            tap("kn", c, kn_tm[:], ev[f"kn_ev_{c}"])
            wt("tensor", ev[f"qn_ev_{c}"], ev[f"mix_v_{c}"], ev_cast["wv"])
            for half in range(2):
                ev[f"v_pe{half}_{c}"] = do("tensor", lambda e, h=half: mm_proj(
                    e, pb[h][:, 0:512], mixv, 2, h), "pe")
            wt("tensor", ev[f"kn_ev_{c}"], ev[f"mix_g_{c}"], ev_cast["wg"],
               ev.get(f"final_mul_{c-1}"))
            for half in range(2):
                ev[f"g_pe{half}_{c}"] = do("tensor", lambda e, h=half: mm_proj(
                    e, pb[2 + h][:, 0:512], mixg, 3, h), "pe")

            wt("scalar", ev[f"v_pe1_{c}"])
            if c == 0:
                wt("scalar", ev_ms0)
            vp3 = v_pl[:].rearrange("p (h u) -> p h u", u=257)

            def v_ev(e):
                e.activation(vp3[:, 0:2, 0:256],
                             pb[0][:, 0:512].rearrange("p (h e) -> p h e", e=256), AF.Copy)
                return e.activation(vp3[:, 2:4, 0:256],
                                    pb[1][:, 0:512].rearrange("p (h e) -> p h e", e=256), AF.Copy)
            ev[f"v_ev_{c}"] = do("scalar", v_ev, "act")

            wt("scalar", ev[f"g_pe1_{c}"], ev[f"mix_v_{c}"], ev[f"mix_g_{c}"],
               ev.get(f"attstt_h3_{c-1}"))
            if (("gpre", c) in DBG_TAPS):
                wt("scalar", ev[f"g_pe1_{c}"])
                ev[f"gpre_ev_{c}"] = do("scalar", lambda e: e.activation(
                    scr_st[:, 0:512], pb[2][:, 0:512], AF.Copy), "act")
                tap("gpre", c, scr_st[:, 0:512], ev[f"gpre_ev_{c}"])
            ev[f"tg_{c}"] = do("scalar", lambda e: (
                e.activation(scrB[:, 0:512], pb[2][:, 0:512], AF.Tanh, scale=0.5),
                e.activation(scrB[:, 512:1024], pb[3][:, 0:512], AF.Tanh, scale=0.5))[-1], "act")
            wt("vector", ev[f"tg_{c}"], ev[f"mix_v_{c}"])
            ev[f"ug_{c}"] = do("vector", lambda e: e.tensor_scalar(scrB[:], scrB[:], 1.0,
                                                                  None, ALU.add), "dve")

            wt("vector", ev[f"xi8_{c}"], ev.get(f"txq_pe_{c-1}"))
            ev[f"xqb_{c}"] = do("vector", lambda e: e.tensor_copy(xq_tmb[:], xi8), "dve")
            wt("tensor", ev[f"xqb_{c}"], ev.get(f"kl_ev_{c-1}"))

            def txq_f(e):
                last = None
                for j in range(NB):
                    last = e.transpose(pbf[:, 128 * j:128 * (j + 1)],
                                       xq_tmb[:, 128 * j:128 * (j + 1)], idb[:])
                return last
            ev[f"txq_pe_{c}"] = do("tensor", txq_f, "pe")
            wt("scalar", ev[f"txq_pe_{c}"])
            ev[f"xqfm_ev_{c}"] = do("scalar", lambda e: e.activation(xq_fm[:], pbf[:],
                                                                     AF.Copy), "act")

            tap("vpl", c, v_pl[:], ev[f"v_ev_{c}"])
            tap("ug", c, scrB[:], ev[f"ug_{c}"])
            wt("tensor", ev[f"qn_ev_{c}"], ev[f"kn_ev_{c}"])
            if c == 0:
                wt("tensor", ev["setup_a"])

            def tq_f(e):
                last = None
                for j in range(NB):
                    dpb = pb[4] if j < 4 else pb[5]
                    last = e.transpose(dpb[:, 128 * (j % 4):128 * (j % 4 + 1)],
                                       qn_tm[:, 128 * j:128 * (j + 1)], idf[:])
                return last
            ev[f"tq_pe_{c}"] = do("tensor", tq_f, "pe")
            wt("scalar", ev[f"tq_pe_{c}"], ev.get(f"wkv_qscal_{c-1}"),
               ev.get(f"final_mul_{c-1}"))
            if c == 0:
                wt("scalar", ev["setup_v"])

            def tq_ev(e):
                last = None
                for j in range(NB):
                    dpb = pb[4] if j < 4 else pb[5]
                    last = e.activation(att_tm[:, 128 * j:128 * (j + 1)],
                                        dpb[:, 128 * (j % 4):128 * (j % 4 + 1)],
                                        AF.Exp, scale=col(cvec, 8 + j))
                return last
            ev[f"qexp_{c}"] = do("scalar", tq_ev, "act")

            wt("tensor", ev[f"kn_ev_{c}"], ev[f"v_ev_{c}"])

            def tk_f(e):
                last = None
                for j in range(NB):
                    dpb = pb[0] if j < 4 else pb[1]
                    last = e.transpose(dpb[:, 128 * (j % 4):128 * (j % 4 + 1)],
                                       kn_tm[:, 128 * j:128 * (j + 1)], idf[:])
                return last
            ev[f"tk_pe_{c}"] = do("tensor", tk_f, "pe")
            wt("scalar", ev[f"tk_pe_{c}"], ev.get(f"wkv_kscal_{c-1}"))
            if c == 0:
                wt("scalar", ev["setup_v"])

            def tk_ev(e):
                last = None
                for j in range(NB):
                    dpb = pb[0] if j < 4 else pb[1]
                    last = e.activation(kexp_fm[:, 128 * j:128 * (j + 1)],
                                        dpb[:, 128 * (j % 4):128 * (j % 4 + 1)],
                                        AF.Exp, scale=col(cvec, 8 + j))
                return last
            ev[f"kexpfm_{c}"] = do("scalar", tk_ev, "act")

            tap("xqfm", c, xq_fm[:], ev[f"xqfm_ev_{c}"])
            wt("tensor", ev[f"xqfm_ev_{c}"], ev[f"tg_{c}"], ev_cast["wf"])
            for half in range(2):
                ev[f"ffn_pe{half}_{c}"] = do("tensor", lambda e, h=half: mm_proj(
                    e, pb[2 + h][:, 0:512], xq_fm, 4, h), "pe")
            wt("scalar", ev[f"ffn_pe1_{c}"])
            ev[f"ffny_ev_{c}"] = do("scalar", lambda e: (
                e.activation(kn_tm[:, 0:512], pb[2][:, 0:512], AF.Copy, scale=cc("r_t", c)),
                e.activation(kn_tm[:, 512:1024], pb[3][:, 0:512], AF.Copy,
                             scale=cc("r_t", c)))[-1], "act")

            tap("qexp", c, att_tm[:], ev[f"qexp_{c}"])
            tap("kexp", c, kexp_fm[:], ev[f"kexpfm_{c}"])
            PQT_, PQN_, PKN_, PQF_, PKF_, PKLf_ = [Pt[:, i * C:(i + 1) * C] for i in range(6)]
            wt("vector", ev[f"qexp_{c}"], ev.get(f"S_pe_h3_{c-1}"), ev.get(f"sc_pe_{c-1}"),
               ev.get(f"txq_pe_{c-1}"))

            ev[f"wkv_qt_{c}"] = do("vector", lambda e: e.tensor_tensor(
                qt_b[:], att_tm[:], PQT_, ALU.mult), "dve")
            def qscal_g(e):
                e.tensor_tensor(qn_b[:], att_tm[:], PQN_, ALU.mult)
                return e.tensor_tensor(qf_b[:], att_tm[:], PQF_, ALU.mult)
            ev[f"wkv_qscal_{c}"] = do("vector", qscal_g, "dve")
            wt("vector", ev[f"kexpfm_{c}"], ev.get(f"st_pe_h3_{c-1}"),
               ev.get(f"wo_pe1_{c-1}"))

            def kscal_wf(e):
                e.tensor_tensor(kn_b[:], kexp_fm[:], PKN_, ALU.mult)
                return e.tensor_tensor(kl_b[:], kexp_fm[:], PKLf_, ALU.mult)
            ev[f"wkv_kscal_{c}"] = do("vector", kscal_wf, "dve")
            ev[f"wkv_kf_{c}"] = do("vector", lambda e: e.tensor_tensor(
                kf_b[:], kexp_fm[:], PKF_, ALU.mult), "dve")
            wt("vector", ev.get(f"sc_pe_{c-1}"))

            def ka_f(e):
                last = None
                for j in range(NB):
                    last = e.tensor_scalar(ka_b[:, 128 * j:128 * (j + 1)],
                                           kn_b[:, 128 * j:128 * (j + 1)],
                                           col(cdec, 36 + j), None, ALU.mult)
                return last
            ev[f"wkv_ka_{c}"] = do("vector", ka_f, "dve")

            wt("tensor", ev[f"wkv_kscal_{c}"], ev[f"xqfm_ev_{c}"])

            def tkl_f(e):
                last = None
                for j in range(NB):
                    last = e.transpose(pbf[:, 128 * j:128 * (j + 1)],
                                       kl_b[:, 128 * j:128 * (j + 1)], idb[:])
                return last
            ev[f"tkl_pe_{c}"] = do("tensor", tkl_f, "pe")
            wt("scalar", ev[f"tkl_pe_{c}"], ev.get(f"st_pe_h3_{c-1}"))
            ev[f"kl_ev_{c}"] = do("scalar", lambda e: e.activation(kl_b[:], pbf[:],
                                                                   AF.Copy), "act")

            tap("qtb", c, qt_b[:], ev[f"wkv_qt_{c}"])
            tap("qnb", c, qn_b[:], ev[f"wkv_qscal_{c}"])
            tap("knb", c, kn_b[:], ev[f"wkv_kscal_{c}"])
            tap("kab", c, ka_b[:], ev[f"wkv_ka_{c}"])
            tap("kfb", c, kf_b[:], ev[f"wkv_kf_{c}"])
            tap("klb", c, kl_b[:], ev[f"kl_ev_{c}"])
            wt("tensor", ev[f"wkv_qscal_{c}"], ev[f"wkv_kscal_{c}"], ev[f"wkv_ka_{c}"],
               ev[f"wkv_qt_{c}"], ev[f"wkv_kf_{c}"], ev[f"qexp_{c}"], ev[f"ffny_ev_{c}"])

            def sc_f(e):
                last = None
                for hh in range(H):
                    for (dpb, kb, qb) in ((pb[4], kf_b, qf_b), (pb[5], kn_b, qn_b),
                                          (pb[2], ka_b, qn_b)):
                        for jj in range(2):
                            j = 2 * hh + jj
                            last = e.matmul(dpb[:, 128 * hh:128 * (hh + 1)],
                                            kb[:, 128 * j:128 * (j + 1)],
                                            qb[:, 128 * j:128 * (j + 1)],
                                            start=(jj == 0), stop=(jj == 1))
                return last
            ev[f"sc_pe_{c}"] = do("tensor", sc_f, "pe")

            wt("vector", ev[f"sc_pe_{c}"], ev.get(f"tatt_pe_{c-1}"))

            def sc_min(e):
                e.tensor_tensor(att_tm[:, 0:512], pb[4][:, 0:512], MM4[:, 1024:1536], ALU.min)
                e.tensor_tensor(att_tm[:, 512:1024], pb[5][:, 0:512], MM4[:, 0:512], ALU.min)
                e.tensor_tensor(qn_tm[:, 0:512], pb[2][:, 0:512], MM4[:, 512:1024], ALU.min)
                e.drain()
                e.tensor_tensor(att_tm[:, 0:512], att_tm[:, 0:512], att_tm[:, 512:1024], ALU.add)
                e.drain()
                return e.tensor_tensor(sc_b[:], att_tm[:, 0:512], qn_tm[:, 0:512], ALU.add)
            ev[f"scmin_{c}"] = do("vector", sc_min, "dve")

            wt("vector", ev[f"ffny_ev_{c}"])
            ev[f"ybias_{c}"] = do("vector", lambda e: e.tensor_tensor(
                kn_tm[:], kn_tm[:], BIASB[:], ALU.add), "dve")

            tap("scb", c, sc_b[:], ev[f"scmin_{c}"])
            tap("colsb", c, cols[:], ev[f"scmin_{c}"])
            for hh in range(H):
                sb_bank = pb[3] if hh % 2 == 0 else pb[6]
                wt("tensor", ev[f"scmin_{c}"], ev.get(f"stb_{c-1}"))
                if hh == 0:
                    wt("tensor", ev[f"tg_{c}"], ev[f"v_ev_{c}"])
                    if c == 0:
                        wt("tensor", ev_ms1)
                elif hh == 1:
                    wt("tensor", ev[f"txn_ev_{c}"])
                else:
                    wt("tensor", ev[f"attstt_h{hh-2}_{c}"])

                def S_f(e, hh=hh, sb_bank=sb_bank):
                    for jj in range(2):
                        j = 2 * hh + jj
                        e.matmul(sb_bank[:, 0:257], qt_b[:, 128 * j:128 * (j + 1)],
                                 st_b[:, 257 * j:257 * (j + 1)], start=(jj == 0), stop=False)
                    return e.matmul(sb_bank[:, 0:257], sc_b[:, 128 * hh:128 * (hh + 1)],
                                    v_pl[:, 257 * hh:257 * (hh + 1)], start=False, stop=True)
                ev[f"S_pe_h{hh}_{c}"] = do("tensor", S_f, "pe")

                if hh == 0:
                    wt("tensor", ev[f"kexpfm_{c}"], ev[f"kl_ev_{c}"])
                elif hh == 1:
                    wt("tensor", ev[f"scmin_{c}"])
                else:
                    wt("tensor", ev[f"stv_h{hh-2}_{c}"])
                sbk = (pb[0], pb[1]) if hh % 2 == 0 else (pb[4], pb[5])

                def st_pe(e, hh=hh, sbk=sbk):
                    e.matmul(sbk[0][:, 0:257], kl_b[:, 128 * (2 * hh):128 * (2 * hh + 1)],
                             v_pl[:, 257 * hh:257 * (hh + 1)], start=True, stop=True)
                    return e.matmul(sbk[1][:, 0:257],
                                    kl_b[:, 128 * (2 * hh + 1):128 * (2 * hh + 2)],
                                    v_pl[:, 257 * hh:257 * (hh + 1)], start=True, stop=True)
                ev[f"st_pe_h{hh}_{c}"] = do("tensor", st_pe, "pe")

                wt("vector", ev[f"S_pe_h{hh}_{c}"], ev[f"ug_{c}"], ev[f"d1_{c}"])
                do("vector", lambda e, hh=hh, sb_bank=sb_bank: e.reciprocal(
                    cc(f"rz{hh}", c), sb_bank[:, 256:257]), "dve")
                if hh % 2 == 1:
                    def att_pair(e, hh=hh):
                        e.drain()
                        last = None
                        for h2 in (hh - 1, hh):
                            sbk2 = pb[3] if h2 % 2 == 0 else pb[6]
                            last = e.scalar_tensor_tensor(
                                att_tm[:, 256 * h2:256 * (h2 + 1)],
                                scrB[:, 256 * h2:256 * (h2 + 1)],
                                cc(f"rz{h2}", c),
                                sbk2[:, 0:256], ALU.mult, ALU.mult)
                        return last
                    epair = do("vector", att_pair, "dve")
                    ev[f"attstt_h{hh-1}_{c}"] = epair
                    ev[f"attstt_h{hh}_{c}"] = epair

                wt("scalar", ev.get(f"stv_h{hh-2}_{c}") if hh > 1 else None)
                if c == 0 and hh == 0:
                    wt("scalar", ev_ms1)

                def st_sc(e, hh=hh):
                    bo = 514 * (hh % 2)
                    e.activation(scr_st[:, bo:bo + 257],
                                 st_f[:, 257 * (2 * hh):257 * (2 * hh) + 257],
                                 AF.Copy, scale=col(cdec, 24 + 2 * hh))
                    return e.activation(scr_st[:, bo + 257:bo + 514],
                                        st_f[:, 257 * (2 * hh + 1):257 * (2 * hh + 1) + 257],
                                        AF.Copy, scale=col(cdec, 24 + 2 * hh + 1))
                ev[f"st_sc_h{hh}_{c}"] = do("scalar", st_sc, "act")
                wt("vector", ev[f"st_sc_h{hh}_{c}"], ev[f"st_pe_h{hh}_{c}"],
                   ev[f"S_pe_h{hh}_{c}"])

                def st_v(e, hh=hh, sbk=sbk):
                    bo = 514 * (hh % 2)
                    e.tensor_tensor(st_f[:, 257 * (2 * hh):257 * (2 * hh) + 257],
                                    scr_st[:, bo:bo + 257], sbk[0][:, 0:257], ALU.add)
                    return e.tensor_tensor(st_f[:, 257 * (2 * hh + 1):257 * (2 * hh + 1) + 257],
                                           scr_st[:, bo + 257:bo + 514], sbk[1][:, 0:257],
                                           ALU.add)
                ev[f"stv_h{hh}_{c}"] = do("vector", st_v, "dve")

            wt("scalar", ev[f"ybias_{c}"], ev[f"scmin_{c}"])
            ev[f"th_{c}"] = do("scalar", lambda e: e.activation(qn_tm[:], kn_tm[:], AF.Tanh,
                                                                scale=0.5), "act")

            tap("att", c, att_tm[:], ev[f"attstt_h3_{c}"])
            tap("colsc", c, cols[:], ev[f"stv_h3_{c}"])
            tap("stf", c, st_f[:], ev[f"stv_h3_{c}"])
            if c + 1 < nch:
                tail(c + 1)

            wt("scalar", ev[f"stv_h3_{c}"])
            ev[f"stb_{c}"] = do("scalar", lambda e: e.activation(
                st_b[:], st_f[:], AF.Copy), "act")

            wt("tensor", ev[f"attstt_h3_{c}"], ev[f"scmin_{c}"], ev[f"stv_h3_{c}"])

            def tatt_f(e):
                last = None
                for j in range(NB):
                    dpb = pb[4] if j < 4 else pb[5]
                    last = e.transpose(dpb[:, 128 * (j % 4):128 * (j % 4 + 1)],
                                       att_tm[:, 128 * j:128 * (j + 1)], idf[:])
                return last
            ev[f"tatt_pe_{c}"] = do("tensor", tatt_f, "pe")
            wt("scalar", ev[f"tatt_pe_{c}"], ev[f"wkv_ka_{c}"])
            ev[f"tatt_ev_{c}"] = do("scalar", lambda e: (
                e.activation(kn_b[:, 0:512], pb[4][:, 0:512], AF.Copy),
                e.activation(kn_b[:, 512:1024], pb[5][:, 0:512], AF.Copy))[-1], "act")

            tap("attfm", c, kn_b[:], ev[f"tatt_ev_{c}"])
            wt("vector", ev[f"th_{c}"], ev[f"tatt_pe_{c}"])
            ev[f"gf_{c}"] = do("vector", lambda e: e.scalar_tensor_tensor(
                att_tm[:], qn_tm[:], 1.0, kn_tm[:], ALU.add, ALU.mult), "dve")

            tap("yffn", c, kn_tm[:], ev[f"ybias_{c}"])
            tap("gf", c, att_tm[:], ev[f"gf_{c}"])
            tap("cols", c, cols[:], ev[f"gf_{c}"])
            wt("tensor", ev[f"tatt_ev_{c}"], ev_cast["wo"])
            if c + 1 < nch:
                wt("tensor", ev[f"txn_ev_{c+1}"])
            for half in range(2):
                ev[f"wo_pe{half}_{c}"] = do("tensor", lambda e, h=half: mm_proj(
                    e, pb[2 + h][:, 0:512], kn_b, 5, h), "pe")

            wt("vector", ev[f"wo_pe1_{c}"], ev[f"gf_{c}"], ev.get(f"store_{c-2}"),
               ev[f"mix_g_{c}"])

            def fin_f(e):
                e.tensor_tensor(out_sb[p][:, 0:512], pb[2][:, 0:512],
                                att_tm[:, 0:512], ALU.mult)
                e.tensor_tensor(out_sb[p][:, 512:1024], pb[3][:, 0:512],
                                att_tm[:, 512:1024], ALU.mult)
                e.drain()
                return e.tensor_tensor(out_sb[p][:], out_sb[p][:], x_tm[p][:], ALU.add)
            ev[f"final_mul_{c}"] = do("vector", fin_f, "dve")

            wt("scalar", ev[f"final_mul_{c}"])
            ev[f"store_{c}"] = do("scalar", lambda e, c=c: e.dma_start(
                out=y_d[128 * c:128 * (c + 1), :], in_=out_sb[c & 1][:]),
                ("st0" if c % 2 == 0 else "st1", 16))

            if c + 2 < nch:
                wt("sync", ev[f"final_mul_{c}"])
                ev_ldx[c + 2] = do("sync", lambda e, c=c: e.dma_start(
                    out=x_tm[c & 1][:], in_=x_d[128 * (c + 2):128 * (c + 3), :]),
                    ("lx0" if c % 2 == 0 else "lx1", 16))

        tail(0, first=True)
        for c in range(nch):
            chunk_body(c)

        P.wait("gpsimd", ("st0", P.cnt["st0"]))
        P.wait("gpsimd", ("st1", P.cnt["st1"]))

        block = es.enter_context(nc.Block())
        P.emit(block)
    return nc


def kernel(**inputs):
    x = np.ascontiguousarray(inputs["L0_attn_x"], dtype=np.float32)
    B, T, _ = x.shape
    nch = T // LC
    wT = {
        "wq": np.ascontiguousarray(np.asarray(inputs["L0_attn_Wq"], np.float32).T),
        "wk": np.ascontiguousarray(np.asarray(inputs["L0_attn_Wk"], np.float32).T),
        "wv": np.ascontiguousarray(np.asarray(inputs["L0_attn_Wv"], np.float32).T),
        "wg": np.ascontiguousarray(np.asarray(inputs["L0_attn_Wg"], np.float32).T),
        "wf": np.ascontiguousarray(np.asarray(inputs["L0_ffn_w_int8"], np.int32).T),
        "wo": np.ascontiguousarray(np.asarray(inputs["L0_attn_Wo"], np.float32).T),
    }
    vecs = np.zeros((8, C), np.float32)
    vecs[0] = np.asarray(inputs["L0_attn_time_decay"], np.float32)
    vecs[1] = np.asarray(inputs["L0_attn_norm_scale"], np.float32)
    vecs[2] = np.asarray(inputs["L0_attn_mix_q"], np.float32).reshape(-1)
    vecs[3] = np.asarray(inputs["L0_attn_mix_k"], np.float32).reshape(-1)
    vecs[4] = np.asarray(inputs["L0_attn_mix_v"], np.float32).reshape(-1)
    vecs[5] = np.asarray(inputs["L0_attn_mix_g"], np.float32).reshape(-1)
    vecs[6] = np.asarray(inputs["L0_ffn_bias"], np.float32)
    vecs[7, 0] = np.float32(np.asarray(inputs["L0_ffn_weight_scale"]))

    nc = build_program(nch)
    in_maps = []
    for b in range(B):
        m = {"x": x[b]}
        m.update(wT)
        m["vecs"] = vecs
        in_maps.append(m)
    res = run_bass_kernel_spmd(nc, in_maps, core_ids=list(range(B)))
    if DBG_TAPS:
        import pickle
        dbg = [{k: r[k] for k in r if k.startswith("dbg_")} for r in res.results]
        with open("/tmp/ktaps.pkl", "wb") as f:
            pickle.dump(dbg, f)
    return np.stack([r["y"] for r in res.results]).astype(np.float32)




# version 17 (final)
# speedup vs baseline: 1.0393x; 1.0393x over previous
import contextlib
import numpy as np
import concourse.bass as bass
import concourse.mybir as mybir
from concourse.bass_utils import run_bass_kernel_spmd

F32 = mybir.dt.float32
BF16 = mybir.dt.bfloat16
I8 = mybir.dt.int8
I32 = mybir.dt.int32
AF = mybir.ActivationFunctionType
ALU = mybir.AluOpType
AX = mybir.AxisListType

C = 1024
NB = 8
H = 4
LC = 128
CL_NEAR = -8.0
TH_FAR = 0.4
WN = ["wq", "wk", "wv", "wg", "wf", "wo"]

DBG_TAPS = []

ENGS = ("sync", "tensor", "vector", "scalar", "gpsimd")
OWNER = {"pe": "tensor", "dve": "vector", "act": "scalar", "pool": "gpsimd"}


class Prog:
    def __init__(self, nc, es):
        self.nc = nc
        self.q = {e: [] for e in ENGS}
        names = ("pe", "dve", "act", "pool", "lx0", "lx1", "lw0", "lw1", "lwi", "lv", "st0", "st1")
        self.sem = {n: es.enter_context(nc.semaphore("s_" + n)) for n in names}
        self.cnt = {n: 0 for n in names}
        self.wmax = {}
        self.waited = {}

    def do(self, eng, fn, sig=None):
        amt, name = 0, None
        force = False
        if sig is not None:
            name, amt = (sig, 1) if isinstance(sig, str) else sig
            if name.endswith("!"):
                force, name = True, name[:-1]
            self.cnt[name] += amt
        if force or name == "pool":
            self.q[eng].append(("g", fn, name, amt, -1))
        else:
            self.q[eng].append(("x", fn, name, amt))
        return (name, self.cnt[name]) if name else None

    def wait(self, eng, *evs):
        for ev in evs:
            if ev is None:
                continue
            name, v = ev
            if OWNER.get(name) == eng:
                continue
            key = (eng, name)
            if self.wmax.get(key, 0) >= v:
                continue
            self.wmax[key] = v
            self.waited.setdefault(name, set()).add(v)
            self.q[eng].append(("w", name, v, 0))

    def emit(self, block):
        for ename in ENGS:
            deco = getattr(block, ename)
            ops = self.q[ename]

            def body(eng, _ops=ops):
                for op in _ops:
                    kind = op[0]
                    if kind == "w":
                        eng.wait_ge(self.sem[op[1]], op[2])
                    elif kind == "g":
                        _, fn, name, amt, cum = op
                        fn(eng)
                        eng.maybe_drain_then_inc((self.sem[name], amt), fusable=False)
                    else:
                        _, fn, name, amt = op
                        inst = fn(eng)
                        if name is not None:
                            assert inst is not None
                            inst.then_inc(self.sem[name], amt)
            deco(body)


def build_program(nch):
    T = nch * LC
    nc = bass.Bass("TRN2", target_bir_lowering=False, dynamic_dma_scratch_size=2048)
    x_d = nc.dram_tensor("x", [T, C], F32, kind="ExternalInput").ap()
    w_d = {w: nc.dram_tensor(w, [C, C], I32 if w == "wf" else F32,
                             kind="ExternalInput").ap() for w in WN}
    vec_d = nc.dram_tensor("vecs", [8, C], F32, kind="ExternalInput").ap()
    y_d = nc.dram_tensor("y", [T, C], F32, kind="ExternalOutput").ap()

    es = contextlib.ExitStack()
    with es:
        def sbt(name, shape, dt=F32):
            return es.enter_context(nc.sbuf_tensor(name, shape, dt))

        wbf = sbt("wbf", [128, 6 * NB * C], BF16)
        Pt = sbt("Pt", [128, 6 * C])
        BIASB = sbt("BIASB", [128, C])
        MM4 = sbt("MM4", [128, 3 * 512])
        idf = sbt("idf", [128, 128])
        idb = sbt("idb", [128, 128], BF16)
        cvec = sbt("cvec", [128, 48])
        cdec = sbt("cdec", [128, 48])
        cols = sbt("cols", [128, 192])
        st_f = sbt("st_f", [128, 8 * 257])
        st_b = sbt("st_b", [128, 8 * 257], BF16)
        scr_st = sbt("scr_st", [128, 4 * 257])
        x_tm = [sbt("x_tm0", [128, C]), sbt("x_tm1", [128, C])]
        xn_tm = sbt("xn_tm", [128, C])
        xn_fm = sbt("xn_fm", [128, C])
        mixq = sbt("mixq", [128, C], BF16)
        mixk = sbt("mixk", [128, C], BF16)
        mixv = sbt("mixv", [128, C], BF16)
        mixg = sbt("mixg", [128, C], BF16)
        scrB = sbt("scrB", [128, C])
        qn_tm = sbt("qn_tm", [128, C])
        kn_tm = sbt("kn_tm", [128, C])
        kexp_fm = sbt("kexp_fm", [128, C])
        att_tm = sbt("att_tm", [128, C])
        qt_b = sbt("qt_b", [128, C], BF16)
        qn_b = sbt("qn_b", [128, C], BF16)
        kn_b = sbt("kn_b", [128, C], BF16)
        ka_b = sbt("ka_b", [128, C], BF16)
        qf_b = sbt("qf_b", [128, C], BF16)
        kf_b = sbt("kf_b", [128, C], BF16)
        xq_tmb = sbt("xq_tmb", [128, C], BF16)
        kl_b = sbt("kl_b", [128, C], BF16)
        sc_b = sbt("sc_b", [128, 512], BF16)
        xq_fm = sbt("xq_fm", [128, C], BF16)
        v_pl = sbt("v_pl", [128, 4 * 257], BF16)
        xi8 = sc_b[:].bitcast(I8)
        out_sb = [sbt("out0", [128, C]), sbt("out1", [128, C])]
        prevl = sbt("prevl", [128, 8])
        pb = [es.enter_context(nc.psum_tensor(f"pb{i}", [128, 512], F32)) for i in range(7)]
        pbf = es.enter_context(nc.psum_tensor("pb7", [128, 1024], BF16))

        vrow = kexp_fm

        P = Prog(nc, es)
        do, wt = P.do, P.wait
        ev = {}
        dbg_outs = {}

        def tap(tag, c, buf, evt, eng="scalar"):
            if (tag, c) not in DBG_TAPS:
                return
            name = f"dbg_{tag}_{c}"
            shp = list(buf.shape)
            dt_ = buf.dtype
            od = nc.dram_tensor(name, shp, dt_, kind="ExternalOutput").ap()
            dbg_outs[name] = od
            wt(eng, evt)
            do(eng, lambda e, od=od, buf=buf: e.dma_start(out=od, in_=buf), ("st0", 16))

        def col(t, i):
            return t[:, i:i + 1]

        CN = ("sx", "sq", "m", "var", "rstd", "nmr", "rf", "amax", "tq",
              "qmul", "r_t", "qs0", "qs1", "rq", "ks0", "ks1", "rk",
              "rz0", "rz1", "rz2", "rz3", "t0", "t1", "t2")

        def cc(name, c):
            return col(cols, 4 * CN.index(name) + 96 * (c & 1))

        ev_ldx = [None] * (nch + 2)
        ev_ldx[0] = do("sync", lambda e: e.dma_start(out=x_tm[0][:], in_=x_d[0:128, :]), ("lx0", 16))
        if nch > 1:
            ev_ldx[1] = do("sync", lambda e: e.dma_start(out=x_tm[1][:], in_=x_d[128:256, :]), ("lx1", 16))

        ev_ms0 = do("gpsimd", lambda e: (e.memset(prevl[:], 0.0), e.memset(v_pl[:], 1.0),
                                         e.memset(cols[:], -333.25),
                                         e.memset(att_tm[0:1, 512:640], 1.0))[-1], "pool")

        ev_cast = {}
        ev_stg_free = [None, None]
        cast_eng = {"wq": "scalar", "wk": "scalar", "wv": "vector", "wg": "vector",
                    "wf": "gpsimd", "wo": "scalar"}
        cast_sem = {"scalar": "act", "vector": "dve", "gpsimd": "pool"}
        stg = [st_f[:, 0:1024], st_f[:, 1024:2048]]
        for wi, w in enumerate(WN):
            eng = cast_eng[w]
            for j in range(NB):
                src = w_d[w][128 * j:128 * (j + 1), :]
                dst = wbf[:, (wi * NB + j) * C:(wi * NB + j + 1) * C]
                ldq = "sync"
                if w == "wf":
                    sp = j & 1
                    wt(ldq, ev_stg_free[sp])
                    ev_ld = do(ldq, lambda e, s=src, sp=sp: e.dma_start(
                        out=stg[sp].bitcast(I32), in_=s),
                        ("lw0" if sp == 0 else "lw1", 16))
                    wt(eng, ev_ld)
                    ev_c = do(eng, lambda e, d=dst, sp=sp: e.tensor_copy(
                        d, stg[sp].bitcast(I32)), cast_sem[eng])
                    ev_stg_free[sp] = ev_c
                else:
                    sp = j & 1
                    wt(ldq, ev_stg_free[sp])
                    ev_ld = do(ldq, lambda e, s=src, sp=sp: e.dma_start(out=stg[sp], in_=s),
                               ("lw0" if sp == 0 else "lw1", 16))
                    wt(eng, ev_ld)
                    if eng == "scalar":
                        sc = 0.25 if w == "wo" else 1.0
                        ev_c = do(eng, lambda e, d=dst, sp=sp, sc=sc:
                                  e.activation(d, stg[sp], AF.Copy, scale=sc), "act")
                    else:
                        ev_c = do(eng, lambda e, d=dst, sp=sp:
                                  e.tensor_copy(d, stg[sp]), cast_sem[eng])
                    ev_stg_free[sp] = ev_c
            ev_cast[w] = ev_c

        def iota_f(e):
            e.iota(att_tm[:, 0:128], pattern=[[1, 128]], base=0,
                   channel_multiplier=0, allow_small_or_imprecise_dtypes=True)
            e.iota(att_tm[:, 128:256], pattern=[[0, 8], [1, 16]], base=0,
                   channel_multiplier=0, allow_small_or_imprecise_dtypes=True)
            e.iota(att_tm[:, 256:384], pattern=[[1, 8], [0, 16]], base=0,
                   channel_multiplier=0, allow_small_or_imprecise_dtypes=True)
            e.iota(col(cdec, 32), pattern=[[0, 1]], base=0,
                   channel_multiplier=1, allow_small_or_imprecise_dtypes=True)
            e.iota(col(cdec, 33), pattern=[[0, 1]], base=127,
                   channel_multiplier=-1, allow_small_or_imprecise_dtypes=True)
            e.iota(kn_tm[:, 0:512], pattern=[[0, 4], [1, 128]], base=0,
                   channel_multiplier=0, allow_small_or_imprecise_dtypes=True)
            return e.iota(kn_tm[:, 512:1024], pattern=[[0, 4], [1, 8], [0, 16]], base=0,
                          channel_multiplier=0, allow_small_or_imprecise_dtypes=True)
        ev_iota = do("gpsimd", iota_f, "pool")

        wt("vector", ev_iota)
        ev_id = do("vector", lambda e: e.tensor_scalar(idf[:], att_tm[:, 0:128],
                                                       col(cdec, 32), None, ALU.is_equal), "dve")
        ev_idb = do("vector", lambda e: e.tensor_copy(idb[:], idf[:]), "dve")

        wt("tensor", ev_id)
        ev_bs = do("tensor", lambda e: e.transpose(pb[6][:, 0:1], att_tm[0:1, 256:384],
                                                   idf[0:1, 0:1]), "pe")
        wt("vector", ev_bs)
        ev_bscp = do("vector", lambda e: e.tensor_copy(col(cdec, 34), pb[6][:, 0:1]), "dve!")

        def mk_masks(e):
            e.tensor_scalar(qn_tm[:, 0:512], kn_tm[:, 0:512], col(cdec, 32), None, ALU.subtract)
            e.tensor_scalar(kn_tm[:, 512:1024], kn_tm[:, 512:1024], col(cdec, 34), None,
                            ALU.subtract)
            e.drain()
            e.tensor_scalar(MM4[:, 0:512], kn_tm[:, 512:1024], 0.0, 1e30, ALU.is_equal, ALU.mult)
            e.tensor_scalar(qn_tm[:, 0:512], qn_tm[:, 0:512], -0.5, None, ALU.is_ge)
            e.drain()
            e.tensor_tensor(MM4[:, 0:512], MM4[:, 0:512], qn_tm[:, 0:512], ALU.mult)
            e.tensor_scalar(MM4[:, 512:1024], kn_tm[:, 512:1024], 1.0, 1e30, ALU.is_equal,
                            ALU.mult)
            return e.tensor_scalar(MM4[:, 1024:1536], kn_tm[:, 512:1024], 1.5, 1e30,
                                   ALU.is_ge, ALU.mult)
        ev_mm4 = do("vector", mk_masks, "dve")

        vstg = kexp_fm[0:1, :]
        ev_vtr = ev_bscp
        for vi in range(6):
            wt("sync", ev_vtr)
            ev_ldvi = do("sync", lambda e, vi=vi: e.dma_start(out=vstg, in_=vec_d[vi:vi + 1, :]),
                         ("lv", 16))
            wt("tensor", ev_ldvi, ev_bscp)

            def tr_f(e, vi=vi):
                last = None
                for j in range(NB):
                    last = e.transpose(pb[6][:, vi * 8 + j: vi * 8 + j + 1],
                                       vstg[0:1, 128 * j:128 * (j + 1)], idf[0:1, 0:1])
                return last
            ev_vtr = do("tensor", tr_f, "pe")
        wt("vector", ev_vtr)
        ev_cvec = do("vector", lambda e: e.tensor_copy(cvec[:], pb[6][:, 0:48]), "dve!")

        wt("scalar", ev_cvec)
        ev_d1s = do("scalar", lambda e: e.activation(cdec[:, 0:8], cvec[:, 0:8], AF.Exp), "act!")
        wt("vector", ev_d1s)

        def dec2(e):
            e.tensor_scalar(cdec[:, 0:8], cdec[:, 0:8], -1.0, None, ALU.mult)
            e.drain()
            e.tensor_scalar(cdec[:, 8:16], cdec[:, 0:8], CL_NEAR, None, ALU.max)
            return e.tensor_scalar(cdec[:, 16:24], cdec[:, 0:8], float(np.log(TH_FAR)),
                                   None, ALU.max)
        ev_d2 = do("vector", dec2, "dve!")
        wt("scalar", ev_d2)
        ev_wlc = do("scalar", lambda e: (
            e.activation(cdec[:, 24:32], cdec[:, 0:8], AF.Exp, scale=float(LC)),
            e.activation(cdec[:, 36:44], cdec[:, 8:16], AF.Exp, scale=16.0))[-1], "act!")

        RSPEC = [(0, 0, 1.0, 1.0, 0),
                 (1, 1, 1.0, -7.5, 8),
                 (2, 1, -1.0, 7.5, 8),
                 (3, 0, 1.0, -63.5, 16),
                 (4, 0, -1.0, 63.5, 16),
                 (5, 0, -1.0, 127.0, 0)]
        ev_p = None
        for (pi, rsel, ma, ab, lnb) in RSPEC:
            rsrc = att_tm[:, 0:128] if rsel == 0 else att_tm[:, 128:256]
            dst0 = pi * C

            def ramp_f(e, rsrc=rsrc, ma=ma, ab=ab):
                return e.tensor_scalar(att_tm[:, 384:512], rsrc, ma, ab, ALU.mult, ALU.add)
            do("vector", ramp_f, "dve")

            def blocks_f(e, dst0=dst0, lnb=lnb):
                last = None
                for j in range(NB):
                    last = e.tensor_scalar(Pt[:, dst0 + 128 * j: dst0 + 128 * (j + 1)],
                                           att_tm[:, 384:512], col(cdec, lnb + j), None, ALU.mult)
                return last
            ev_b = do("vector", blocks_f, "dve")
            wt("scalar", ev_b)
            ev_p = do("scalar", lambda e, dst0=dst0: e.activation(
                Pt[:, dst0:dst0 + C], Pt[:, dst0:dst0 + C], AF.Exp), "act")
        ev_pt = ev_p

        wt("sync", ev_vtr)
        ev_ldb = do("sync", lambda e: e.dma_start(out=vstg, in_=vec_d[6:7, :]), ("lv", 16))
        ev_bc_done = None
        ev_bc = None
        for half in range(2):
            sl = slice(512 * half, 512 * (half + 1))
            wt("tensor", ev_ms0, ev_ldb, ev_bc_done)

            def bc_f(e, sl=sl):
                return e.matmul(pb[2][:, 0:512], att_tm[0:1, 512:640], vstg[0:1, sl],
                                start=True, stop=True)
            ev_bc = do("tensor", bc_f, "pe")
            wt("vector", ev_bc, ev_mm4)

            def bc_ev(e, sl=sl):
                return e.tensor_copy(BIASB[:, sl], pb[2][:, 0:512])
            ev_bc_done = do("vector", bc_ev, "dve")
        wt("sync", ev_bc)
        ev_ldw = do("sync", lambda e: e.dma_start(out=vstg[0:1, 0:1], in_=vec_d[7:8, 0:1]),
                    ("lv", 16))
        wt("tensor", ev_ldw)
        ev_bcw = do("tensor", lambda e: e.matmul(pb[3][:, 0:1], att_tm[0:1, 512:640],
                                                 vstg[0:1, 0:1], start=True, stop=True), "pe")
        wt("vector", ev_bcw)
        ev_bc_done = do("vector", lambda e: e.tensor_copy(col(cdec, 35), pb[3][:, 0:1]), "dve!")
        ev["setup_v"] = ev_bc_done
        ev["setup_a"] = ev_pt

        wt("gpsimd", ev_cast["wq"], ev_cast["wk"], ev_cast["wv"], ev_cast["wg"],
           ev_cast["wf"], ev_cast["wo"])
        ev_ms1 = do("gpsimd", lambda e: (e.memset(st_f[:], 0.0),
                                         e.memset(st_b[:], 0.0))[-1], "pool")

        def tail(c, first=False):
            p = c & 1
            xx = out_sb[p]
            wt("vector", ev_ldx[c])
            ev[f"sx{c}"] = do("vector", lambda e: e.reduce_sum(cc("sx", c), x_tm[p][:],
                                                              axis=AX.X), "dve!")
            wt("scalar", ev_ldx[c], ev.get(f"store_{c-2}"))
            if first:
                wt("scalar", ev["setup_v"])
            ev[f"sq{c}"] = do("scalar", lambda e: e.activation(xx[:], x_tm[p][:], AF.Square,
                                                              accum_out=cc("sq", c)), "act!")
            wt("vector", ev[f"sq{c}"])

            def ln_tiny(e):
                e.tensor_scalar(cc("m", c), cc("sx", c), 1.0 / C, None, ALU.mult)
                e.tensor_scalar(cc("t1", c), cc("sx", c), cc("sx", c), 1.0 / (C * C),
                                ALU.mult, ALU.mult)
                e.tensor_scalar(cc("t2", c), cc("sq", c), 1.0 / C, 1e-5, ALU.mult, ALU.add)
                e.drain()
                return e.tensor_tensor(cc("t0", c), cc("t2", c), cc("t1", c), ALU.subtract)
            ev[f"var{c}"] = do("vector", ln_tiny, "dve!")
            wt("scalar", ev[f"var{c}"])

            def rstd_f(e):
                e.activation(cc("t1", c), cc("t0", c), AF.Ln)
                e.drain()
                return e.activation(cc("rstd", c), cc("t1", c), AF.Exp, scale=-0.5)
            ev[f"rstd{c}"] = do("scalar", rstd_f, "act!")
            wt("vector", ev[f"rstd{c}"])

            def nmr_f(e):
                e.tensor_scalar(cc("nmr", c), cc("m", c), cc("rstd", c), -1.0,
                                ALU.mult, ALU.mult)
                e.reciprocal(cc("t1", c), cc("t0", c))
                e.drain()
                return e.tensor_scalar(cc("t2", c), cc("t1", c), -1e-5, 1.000001,
                                       ALU.mult, ALU.add)
            ev[f"nmr{c}"] = do("vector", nmr_f, "dve!")
            wt("scalar", ev[f"nmr{c}"], ev.get(f"amax{c-1}"))

            def xn_f(e):
                e.activation(cc("t1", c), cc("t2", c), AF.Ln)
                e.drain()
                return e.activation(cc("rf", c), cc("t1", c), AF.Exp, scale=-0.5)
            ev[f"xn{c}"] = do("scalar", xn_f, "act!")
            wt("vector", ev[f"xn{c}"])
            wt("vector", ev.get(f"xi8_{c-1}"))
            ev[f"xn{c}"] = do("vector", lambda e: e.tensor_scalar(
                xn_tm[:], x_tm[p][:], cc("rstd", c), cc("nmr", c), ALU.mult, ALU.add), "dve")
            tap("xn", c, xn_tm[:], ev[f"xn{c}"])
            wt("vector", ev[f"xn{c}"])
            ev[f"amax{c}"] = do("vector", lambda e: e.reduce_max(
                cc("amax", c), xn_tm[:], axis=AX.X, apply_absolute_value=True), "dve!")

            def qscal_f(e):
                e.tensor_scalar(cc("tq", c), cc("amax", c), cc("rf", c), 1e-5,
                                ALU.mult, ALU.max)
                e.drain()
                e.tensor_scalar(cc("r_t", c), cc("tq", c), col(cdec, 35), 1.0 / 127.0,
                                ALU.mult, ALU.mult)
                e.reciprocal(cc("t0", c), cc("tq", c))
                e.drain()
                return e.tensor_scalar(cc("qmul", c), cc("t0", c), cc("rf", c), 127.0,
                                       ALU.mult, ALU.mult)
            ev[f"qscal{c}"] = do("vector", qscal_f, "dve!")

            if first:
                wt("tensor", ev[f"xn{c}"], ev_cvec, ev["setup_v"], ev_idb)
            else:
                wt("tensor", ev[f"xn{c}"], ev[f"attstt_h3_{c-1}"], ev[f"scmin_{c-1}"])

            def txn_f(e):
                last = None
                for j in range(NB):
                    dpb = pb[6] if j < 4 else pb[2]
                    last = e.transpose(dpb[:, 128 * (j % 4):128 * (j % 4 + 1)],
                                       xn_tm[:, 128 * j:128 * (j + 1)], idf[:])
                return last
            ev[f"txn_pe_{c}"] = do("tensor", txn_f, "pe")
            wt("scalar", ev[f"txn_pe_{c}"], ev.get(f"mix_v_{c-1}"), ev.get(f"mix_g_{c-1}"))
            ev[f"txn_ev_{c}"] = do("scalar", lambda e: e.activation(
                xn_fm[:, 0:512], pb[6][:, 0:512], AF.Copy), "act")
            wt("vector", ev[f"txn_pe_{c}"], ev.get(f"mix_v_{c-1}"), ev.get(f"mix_g_{c-1}"))
            ev[f"txn_evb_{c}"] = do("vector", lambda e: e.tensor_copy(
                xn_fm[:, 512:1024], pb[2][:, 0:512]), "dve")

            tap("xnfm", c, xn_fm[:], ev[f"txn_ev_{c}"])
            wt("vector", ev[f"txn_ev_{c}"], ev[f"txn_evb_{c}"], ev.get(f"mix_v_{c-1}"),
               ev.get(f"attstt_h3_{c-1}"))

            def xx_f(e):
                for j in range(NB):
                    e.tensor_copy(xx[:, 128 * j + 1:128 * j + 128],
                                  xn_fm[:, 128 * j:128 * j + 127])
                h3b = xx[:].rearrange("p (j t) -> p j t", j=NB)
                return e.tensor_copy(h3b[:, :, 0:1], prevl[:].rearrange("p (j o) -> p j o", o=1))
            ev[f"xx{c}"] = do("vector", xx_f, "dve")

            def pl_f(e):
                x3 = xn_fm[:].rearrange("p (j t) -> p j t", j=NB)
                return e.tensor_copy(prevl[:].rearrange("p (j o) -> p j o", o=1),
                                     x3[:, :, 127:128])
            ev[f"prevl{c}"] = do("vector", pl_f, "dve")

            wt("vector", ev.get(f"g_pe1_{c-1}"))
            ev[f"d1_{c}"] = do("vector", lambda e: e.tensor_tensor(scrB[:], xn_fm[:], xx[:],
                                                                  ALU.subtract), "dve")
            MQ, MK, MV, MG = 16, 24, 32, 40
            mix_out = [(mixq, MQ, "q"), (mixk, MK, "k"), (mixv, MV, "v"), (mixg, MG, "g")]

            def mk_mix(dst, base, j):
                def f(e):
                    sl = slice(128 * j, 128 * (j + 1))
                    return e.scalar_tensor_tensor(dst[:, sl], scrB[:, sl], col(cvec, base + j),
                                                  xx[:, sl], ALU.mult, ALU.add)
                return f
            lastv = None
            for dst, base, tag_ in mix_out:
                for j in range(NB):
                    lastv = do("vector", mk_mix(dst, base, j), "dve")
                ev[f"mix_{tag_}_{c}"] = lastv
            tap("colsa", c, cols[:], ev[f"qscal{c}"])
            tap("wgw", c, wbf[:, 24 * C:32 * C], ev.get(f"g_pe1_{c-1}"))
            tap("wvw", c, wbf[:, 16 * C:24 * C], ev.get(f"g_pe1_{c-1}"))
            tap("cvec", c, cvec[:], ev[f"qscal{c}"])
            tap("xx", c, xx[:], lastv)
            tap("d1", c, scrB[:], lastv)
            tap("mixq", c, mixq[:], lastv)
            tap("mixk", c, mixk[:], lastv)
            tap("mixv", c, mixv[:], lastv)
            tap("mixg", c, mixg[:], lastv)

        def mm_proj(e, psum_half, src_bf, wi, half):
            last = None
            for j in range(NB):
                last = e.matmul(psum_half,
                                src_bf[:, 128 * j:128 * (j + 1)],
                                wbf[:, (wi * NB + j) * C + 512 * half:
                                    (wi * NB + j) * C + 512 * (half + 1)],
                                start=(j == 0), stop=(j == NB - 1))
            return last

        def chunk_body(c):
            p = c & 1
            wt("tensor", ev[f"mix_q_{c}"], ev_cast["wq"], ev.get(f"stv_h2_{c-1}"))
            for half in range(2):
                ev[f"q_pe{half}_{c}"] = do("tensor", lambda e, h=half: mm_proj(
                    e, pb[h][:, 0:512], mixq, 0, h), "pe")
            wt("tensor", ev[f"mix_k_{c}"], ev_cast["wk"], ev.get(f"tatt_ev_{c-1}"))
            for half in range(2):
                ev[f"k_pe{half}_{c}"] = do("tensor", lambda e, h=half: mm_proj(
                    e, pb[4 + h][:, 0:512], mixk, 1, h), "pe")

            wt("scalar", ev[f"q_pe1_{c}"], ev.get(f"gf_{c-1}"))
            ev[f"qsq{c}"] = do("scalar", lambda e: (
                e.activation(out_sb[p][:, 0:512], pb[0][:, 0:512], AF.Square,
                             accum_out=cc("qs0", c)),
                e.activation(out_sb[p][:, 512:1024], pb[1][:, 0:512], AF.Square,
                             accum_out=cc("qs1", c)))[-1], "act!")
            wt("vector", ev[f"qsq{c}"])
            ev[f"rq1{c}"] = do("vector", lambda e: e.tensor_tensor(
                cc("t0", c), cc("qs0", c), cc("qs1", c), ALU.add), "dve!")
            wt("scalar", ev[f"rq1{c}"], ev.get(f"gf_{c-1}"))
            ev[f"rq2{c}"] = do("scalar", lambda e: (
                e.activation(cc("t1", c), cc("t0", c), AF.Ln, scale=1.0 / C),
                e.drain(),
                e.activation(cc("rq", c), cc("t1", c), AF.Exp, scale=-0.5),
                e.drain())[-2], "act")
            if c == 0:
                wt("vector", ev_mm4)
            wt("vector", ev[f"rq2{c}"], ev[f"q_pe1_{c}"], ev.get(f"gf_{c-1}"))
            ev[f"qn_ev_{c}"] = do("vector", lambda e: (
                e.tensor_scalar(qn_tm[:, 0:512], pb[0][:, 0:512], cc("rq", c), None,
                                ALU.mult),
                e.tensor_scalar(qn_tm[:, 512:1024], pb[1][:, 0:512], cc("rq", c), None,
                                ALU.mult))[-1], "dve")

            wt("scalar", ev[f"k_pe1_{c}"])
            ev[f"ksq{c}"] = do("scalar", lambda e: (
                e.activation(out_sb[p][:, 0:512], pb[4][:, 0:512], AF.Square,
                             accum_out=cc("ks0", c)),
                e.activation(out_sb[p][:, 512:1024], pb[5][:, 0:512], AF.Square,
                             accum_out=cc("ks1", c)))[-1], "act!")
            wt("vector", ev[f"ksq{c}"])
            ev[f"rk1{c}"] = do("vector", lambda e: e.tensor_tensor(
                cc("t2", c), cc("ks0", c), cc("ks1", c), ALU.add), "dve!")
            wt("scalar", ev[f"rk1{c}"], ev.get(f"gf_{c-1}"))
            ev[f"rk2{c}"] = do("scalar", lambda e: (
                e.activation(cc("t1", c), cc("t2", c), AF.Ln, scale=1.0 / C),
                e.drain(),
                e.activation(cc("rk", c), cc("t1", c), AF.Exp, scale=-0.5),
                e.drain())[-2], "act")
            wt("vector", ev[f"rk2{c}"], ev[f"k_pe1_{c}"])
            ev[f"kn_ev_{c}"] = do("vector", lambda e: (
                e.tensor_scalar(kn_tm[:, 0:512], pb[4][:, 0:512], cc("rk", c), None,
                                ALU.mult),
                e.tensor_scalar(kn_tm[:, 512:1024], pb[5][:, 0:512], cc("rk", c), None,
                                ALU.mult))[-1], "dve")

            wt("scalar", ev[f"qscal{c}"], ev.get(f"xqb_{c-1}"), ev.get(f"S_pe_h3_{c-1}"))
            ev[f"xi8_{c}"] = do("scalar", lambda e: e.activation(
                xi8, xn_tm[:], AF.Copy, scale=cc("qmul", c)), "act")

            tap("qn", c, qn_tm[:], ev[f"qn_ev_{c}"])
            tap("kn", c, kn_tm[:], ev[f"kn_ev_{c}"])
            wt("tensor", ev[f"qn_ev_{c}"], ev[f"mix_v_{c}"], ev_cast["wv"])
            for half in range(2):
                ev[f"v_pe{half}_{c}"] = do("tensor", lambda e, h=half: mm_proj(
                    e, pb[h][:, 0:512], mixv, 2, h), "pe")
            wt("tensor", ev[f"kn_ev_{c}"], ev[f"mix_g_{c}"], ev_cast["wg"],
               ev.get(f"final_mul_{c-1}"))
            for half in range(2):
                ev[f"g_pe{half}_{c}"] = do("tensor", lambda e, h=half: mm_proj(
                    e, pb[2 + h][:, 0:512], mixg, 3, h), "pe")

            wt("scalar", ev[f"v_pe1_{c}"])
            if c == 0:
                wt("scalar", ev_ms0)
            vp3 = v_pl[:].rearrange("p (h u) -> p h u", u=257)

            def v_ev(e):
                e.activation(vp3[:, 0:2, 0:256],
                             pb[0][:, 0:512].rearrange("p (h e) -> p h e", e=256), AF.Copy)
                return e.activation(vp3[:, 2:4, 0:256],
                                    pb[1][:, 0:512].rearrange("p (h e) -> p h e", e=256), AF.Copy)
            ev[f"v_ev_{c}"] = do("scalar", v_ev, "act")

            wt("scalar", ev[f"g_pe1_{c}"], ev[f"mix_v_{c}"], ev[f"mix_g_{c}"],
               ev.get(f"attstt_h3_{c-1}"))
            if (("gpre", c) in DBG_TAPS):
                wt("scalar", ev[f"g_pe1_{c}"])
                ev[f"gpre_ev_{c}"] = do("scalar", lambda e: e.activation(
                    scr_st[:, 0:512], pb[2][:, 0:512], AF.Copy), "act")
                tap("gpre", c, scr_st[:, 0:512], ev[f"gpre_ev_{c}"])
            ev[f"tg_{c}"] = do("scalar", lambda e: (
                e.activation(scrB[:, 0:512], pb[2][:, 0:512], AF.Tanh, scale=0.5),
                e.activation(scrB[:, 512:1024], pb[3][:, 0:512], AF.Tanh, scale=0.5))[-1], "act")
            wt("vector", ev[f"tg_{c}"], ev[f"mix_v_{c}"])
            ev[f"ug_{c}"] = do("vector", lambda e: e.tensor_scalar(scrB[:], scrB[:], 1.0,
                                                                  None, ALU.add), "dve")

            wt("vector", ev[f"xi8_{c}"], ev.get(f"txq_pe_{c-1}"))
            ev[f"xqb_{c}"] = do("vector", lambda e: e.tensor_copy(xq_tmb[:], xi8), "dve")
            wt("tensor", ev[f"xqb_{c}"], ev.get(f"kl_ev_{c-1}"))

            def txq_f(e):
                last = None
                for j in range(NB):
                    last = e.transpose(pbf[:, 128 * j:128 * (j + 1)],
                                       xq_tmb[:, 128 * j:128 * (j + 1)], idb[:])
                return last
            ev[f"txq_pe_{c}"] = do("tensor", txq_f, "pe")
            wt("scalar", ev[f"txq_pe_{c}"])
            ev[f"xqfm_ev_{c}"] = do("scalar", lambda e: e.activation(xq_fm[:], pbf[:],
                                                                     AF.Copy), "act")

            tap("vpl", c, v_pl[:], ev[f"v_ev_{c}"])
            tap("ug", c, scrB[:], ev[f"ug_{c}"])
            wt("tensor", ev[f"qn_ev_{c}"], ev[f"kn_ev_{c}"])
            if c == 0:
                wt("tensor", ev["setup_a"])

            def tq_f(e):
                last = None
                for j in range(NB):
                    dpb = pb[4] if j < 4 else pb[5]
                    last = e.transpose(dpb[:, 128 * (j % 4):128 * (j % 4 + 1)],
                                       qn_tm[:, 128 * j:128 * (j + 1)], idf[:])
                return last
            ev[f"tq_pe_{c}"] = do("tensor", tq_f, "pe")
            wt("scalar", ev[f"tq_pe_{c}"], ev.get(f"wkv_qscal_{c-1}"),
               ev.get(f"final_mul_{c-1}"))
            if c == 0:
                wt("scalar", ev["setup_v"])

            def tq_ev(e):
                last = None
                for j in range(NB):
                    dpb = pb[4] if j < 4 else pb[5]
                    last = e.activation(att_tm[:, 128 * j:128 * (j + 1)],
                                        dpb[:, 128 * (j % 4):128 * (j % 4 + 1)],
                                        AF.Exp, scale=col(cvec, 8 + j))
                return last
            ev[f"qexp_{c}"] = do("scalar", tq_ev, "act")

            wt("tensor", ev[f"kn_ev_{c}"], ev[f"v_ev_{c}"])

            def tk_f(e):
                last = None
                for j in range(NB):
                    dpb = pb[0] if j < 4 else pb[1]
                    last = e.transpose(dpb[:, 128 * (j % 4):128 * (j % 4 + 1)],
                                       kn_tm[:, 128 * j:128 * (j + 1)], idf[:])
                return last
            ev[f"tk_pe_{c}"] = do("tensor", tk_f, "pe")
            wt("scalar", ev[f"tk_pe_{c}"], ev.get(f"wkv_kscal_{c-1}"))
            if c == 0:
                wt("scalar", ev["setup_v"])

            def tk_ev(e):
                last = None
                for j in range(NB):
                    dpb = pb[0] if j < 4 else pb[1]
                    last = e.activation(kexp_fm[:, 128 * j:128 * (j + 1)],
                                        dpb[:, 128 * (j % 4):128 * (j % 4 + 1)],
                                        AF.Exp, scale=col(cvec, 8 + j))
                return last
            ev[f"kexpfm_{c}"] = do("scalar", tk_ev, "act")

            tap("xqfm", c, xq_fm[:], ev[f"xqfm_ev_{c}"])
            wt("tensor", ev[f"xqfm_ev_{c}"], ev[f"tg_{c}"], ev_cast["wf"])
            for half in range(2):
                ev[f"ffn_pe{half}_{c}"] = do("tensor", lambda e, h=half: mm_proj(
                    e, pb[2 + h][:, 0:512], xq_fm, 4, h), "pe")
            wt("scalar", ev[f"ffn_pe1_{c}"])
            ev[f"ffny_ev_{c}"] = do("scalar", lambda e: (
                e.activation(kn_tm[:, 0:512], pb[2][:, 0:512], AF.Copy, scale=cc("r_t", c)),
                e.activation(kn_tm[:, 512:1024], pb[3][:, 0:512], AF.Copy,
                             scale=cc("r_t", c)))[-1], "act")

            tap("qexp", c, att_tm[:], ev[f"qexp_{c}"])
            tap("kexp", c, kexp_fm[:], ev[f"kexpfm_{c}"])
            PQT_, PQN_, PKN_, PQF_, PKF_, PKLf_ = [Pt[:, i * C:(i + 1) * C] for i in range(6)]
            wt("vector", ev[f"qexp_{c}"], ev.get(f"S_pe_h3_{c-1}"), ev.get(f"sc_pe_{c-1}"),
               ev.get(f"txq_pe_{c-1}"))

            ev[f"wkv_qt_{c}"] = do("vector", lambda e: e.tensor_tensor(
                qt_b[:], att_tm[:], PQT_, ALU.mult), "dve")
            def qscal_g(e):
                e.tensor_tensor(qn_b[:], att_tm[:], PQN_, ALU.mult)
                return e.tensor_tensor(qf_b[:], att_tm[:], PQF_, ALU.mult)
            ev[f"wkv_qscal_{c}"] = do("vector", qscal_g, "dve")
            wt("vector", ev[f"kexpfm_{c}"], ev.get(f"st_pe_h3_{c-1}"),
               ev.get(f"wo_pe1_{c-1}"))

            def kscal_wf(e):
                e.tensor_tensor(kn_b[:], kexp_fm[:], PKN_, ALU.mult)
                return e.tensor_tensor(kl_b[:], kexp_fm[:], PKLf_, ALU.mult)
            ev[f"wkv_kscal_{c}"] = do("vector", kscal_wf, "dve")
            ev[f"wkv_kf_{c}"] = do("vector", lambda e: e.tensor_tensor(
                kf_b[:], kexp_fm[:], PKF_, ALU.mult), "dve")
            wt("vector", ev.get(f"sc_pe_{c-1}"))

            def ka_f(e):
                last = None
                for j in range(NB):
                    last = e.tensor_scalar(ka_b[:, 128 * j:128 * (j + 1)],
                                           kn_b[:, 128 * j:128 * (j + 1)],
                                           col(cdec, 36 + j), None, ALU.mult)
                return last
            ev[f"wkv_ka_{c}"] = do("vector", ka_f, "dve")

            wt("tensor", ev[f"wkv_kscal_{c}"], ev[f"xqfm_ev_{c}"])

            def tkl_f(e):
                last = None
                for j in range(NB):
                    last = e.transpose(pbf[:, 128 * j:128 * (j + 1)],
                                       kl_b[:, 128 * j:128 * (j + 1)], idb[:])
                return last
            ev[f"tkl_pe_{c}"] = do("tensor", tkl_f, "pe")
            wt("scalar", ev[f"tkl_pe_{c}"], ev.get(f"st_pe_h3_{c-1}"))
            ev[f"kl_ev_{c}"] = do("scalar", lambda e: e.activation(kl_b[:], pbf[:],
                                                                   AF.Copy), "act")

            tap("qtb", c, qt_b[:], ev[f"wkv_qt_{c}"])
            tap("qnb", c, qn_b[:], ev[f"wkv_qscal_{c}"])
            tap("knb", c, kn_b[:], ev[f"wkv_kscal_{c}"])
            tap("kab", c, ka_b[:], ev[f"wkv_ka_{c}"])
            tap("kfb", c, kf_b[:], ev[f"wkv_kf_{c}"])
            tap("klb", c, kl_b[:], ev[f"kl_ev_{c}"])
            wt("tensor", ev[f"wkv_qscal_{c}"], ev[f"wkv_kscal_{c}"], ev[f"wkv_ka_{c}"],
               ev[f"wkv_qt_{c}"], ev[f"wkv_kf_{c}"], ev[f"qexp_{c}"], ev[f"ffny_ev_{c}"])

            def sc_f(e):
                last = None
                for hh in range(H):
                    for (dpb, kb, qb) in ((pb[4], kf_b, qf_b), (pb[5], kn_b, qn_b),
                                          (pb[2], ka_b, qn_b)):
                        for jj in range(2):
                            j = 2 * hh + jj
                            last = e.matmul(dpb[:, 128 * hh:128 * (hh + 1)],
                                            kb[:, 128 * j:128 * (j + 1)],
                                            qb[:, 128 * j:128 * (j + 1)],
                                            start=(jj == 0), stop=(jj == 1))
                return last
            ev[f"sc_pe_{c}"] = do("tensor", sc_f, "pe")

            wt("vector", ev[f"sc_pe_{c}"], ev.get(f"tatt_pe_{c-1}"))

            def sc_min(e):
                e.tensor_tensor(att_tm[:, 0:512], pb[4][:, 0:512], MM4[:, 1024:1536], ALU.min)
                e.tensor_tensor(att_tm[:, 512:1024], pb[5][:, 0:512], MM4[:, 0:512], ALU.min)
                e.tensor_tensor(qn_tm[:, 0:512], pb[2][:, 0:512], MM4[:, 512:1024], ALU.min)
                e.drain()
                e.tensor_tensor(att_tm[:, 0:512], att_tm[:, 0:512], att_tm[:, 512:1024], ALU.add)
                e.drain()
                return e.tensor_tensor(sc_b[:], att_tm[:, 0:512], qn_tm[:, 0:512], ALU.add)
            ev[f"scmin_{c}"] = do("vector", sc_min, "dve")

            wt("vector", ev[f"ffny_ev_{c}"])
            ev[f"ybias_{c}"] = do("vector", lambda e: e.tensor_tensor(
                kn_tm[:], kn_tm[:], BIASB[:], ALU.add), "dve")

            tap("scb", c, sc_b[:], ev[f"scmin_{c}"])
            tap("colsb", c, cols[:], ev[f"scmin_{c}"])
            for hh in range(H):
                sb_bank = pb[3] if hh % 2 == 0 else pb[6]
                wt("tensor", ev[f"scmin_{c}"], ev.get(f"stb_{c-1}"))
                if hh == 0:
                    wt("tensor", ev[f"tg_{c}"], ev[f"v_ev_{c}"])
                    if c == 0:
                        wt("tensor", ev_ms1)
                elif hh == 1:
                    wt("tensor", ev[f"txn_ev_{c}"])
                else:
                    wt("tensor", ev[f"attstt_h{hh-2}_{c}"])

                def S_f(e, hh=hh, sb_bank=sb_bank):
                    for jj in range(2):
                        j = 2 * hh + jj
                        e.matmul(sb_bank[:, 0:257], qt_b[:, 128 * j:128 * (j + 1)],
                                 st_b[:, 257 * j:257 * (j + 1)], start=(jj == 0), stop=False)
                    return e.matmul(sb_bank[:, 0:257], sc_b[:, 128 * hh:128 * (hh + 1)],
                                    v_pl[:, 257 * hh:257 * (hh + 1)], start=False, stop=True)
                ev[f"S_pe_h{hh}_{c}"] = do("tensor", S_f, "pe")

                if hh == 0:
                    wt("tensor", ev[f"kexpfm_{c}"], ev[f"kl_ev_{c}"])
                elif hh == 1:
                    wt("tensor", ev[f"scmin_{c}"])
                else:
                    wt("tensor", ev[f"stv_h{hh-2}_{c}"])
                sbk = (pb[0], pb[1]) if hh % 2 == 0 else (pb[4], pb[5])

                def st_pe(e, hh=hh, sbk=sbk):
                    e.matmul(sbk[0][:, 0:257], kl_b[:, 128 * (2 * hh):128 * (2 * hh + 1)],
                             v_pl[:, 257 * hh:257 * (hh + 1)], start=True, stop=True)
                    return e.matmul(sbk[1][:, 0:257],
                                    kl_b[:, 128 * (2 * hh + 1):128 * (2 * hh + 2)],
                                    v_pl[:, 257 * hh:257 * (hh + 1)], start=True, stop=True)
                ev[f"st_pe_h{hh}_{c}"] = do("tensor", st_pe, "pe")

                wt("vector", ev[f"S_pe_h{hh}_{c}"], ev[f"ug_{c}"], ev[f"d1_{c}"])

                def att_ev(e, hh=hh, sb_bank=sb_bank):
                    e.reciprocal(cc(f"rz{hh}", c), sb_bank[:, 256:257])
                    e.drain()
                    return e.scalar_tensor_tensor(att_tm[:, 256 * hh:256 * (hh + 1)],
                                                  scrB[:, 256 * hh:256 * (hh + 1)],
                                                  cc(f"rz{hh}", c),
                                                  sb_bank[:, 0:256], ALU.mult, ALU.mult)
                ev[f"attstt_h{hh}_{c}"] = do("vector", att_ev, "dve")

                wt("scalar", ev.get(f"stv_h{hh-2}_{c}") if hh > 1 else None)
                if c == 0 and hh == 0:
                    wt("scalar", ev_ms1)

                def st_sc(e, hh=hh):
                    bo = 514 * (hh % 2)
                    e.activation(scr_st[:, bo:bo + 257],
                                 st_f[:, 257 * (2 * hh):257 * (2 * hh) + 257],
                                 AF.Copy, scale=col(cdec, 24 + 2 * hh))
                    return e.activation(scr_st[:, bo + 257:bo + 514],
                                        st_f[:, 257 * (2 * hh + 1):257 * (2 * hh + 1) + 257],
                                        AF.Copy, scale=col(cdec, 24 + 2 * hh + 1))
                ev[f"st_sc_h{hh}_{c}"] = do("scalar", st_sc, "act")
                wt("vector", ev[f"st_sc_h{hh}_{c}"], ev[f"st_pe_h{hh}_{c}"],
                   ev[f"S_pe_h{hh}_{c}"])

                def st_v(e, hh=hh, sbk=sbk):
                    bo = 514 * (hh % 2)
                    e.tensor_tensor(st_f[:, 257 * (2 * hh):257 * (2 * hh) + 257],
                                    scr_st[:, bo:bo + 257], sbk[0][:, 0:257], ALU.add)
                    return e.tensor_tensor(st_f[:, 257 * (2 * hh + 1):257 * (2 * hh + 1) + 257],
                                           scr_st[:, bo + 257:bo + 514], sbk[1][:, 0:257],
                                           ALU.add)
                ev[f"stv_h{hh}_{c}"] = do("vector", st_v, "dve")

            wt("scalar", ev[f"ybias_{c}"], ev[f"scmin_{c}"])
            ev[f"th_{c}"] = do("scalar", lambda e: e.activation(qn_tm[:], kn_tm[:], AF.Tanh,
                                                                scale=0.5), "act")

            tap("att", c, att_tm[:], ev[f"attstt_h3_{c}"])
            tap("colsc", c, cols[:], ev[f"stv_h3_{c}"])
            tap("stf", c, st_f[:], ev[f"stv_h3_{c}"])
            if c + 1 < nch:
                tail(c + 1)

            wt("scalar", ev[f"stv_h3_{c}"])
            ev[f"stb_{c}"] = do("scalar", lambda e: e.activation(
                st_b[:], st_f[:], AF.Copy), "act")

            wt("tensor", ev[f"attstt_h3_{c}"], ev[f"scmin_{c}"], ev[f"stv_h3_{c}"])

            def tatt_f(e):
                last = None
                for j in range(NB):
                    dpb = pb[4] if j < 4 else pb[5]
                    last = e.transpose(dpb[:, 128 * (j % 4):128 * (j % 4 + 1)],
                                       att_tm[:, 128 * j:128 * (j + 1)], idf[:])
                return last
            ev[f"tatt_pe_{c}"] = do("tensor", tatt_f, "pe")
            wt("scalar", ev[f"tatt_pe_{c}"], ev[f"wkv_ka_{c}"])
            ev[f"tatt_ev_{c}"] = do("scalar", lambda e: (
                e.activation(kn_b[:, 0:512], pb[4][:, 0:512], AF.Copy),
                e.activation(kn_b[:, 512:1024], pb[5][:, 0:512], AF.Copy))[-1], "act")

            tap("attfm", c, kn_b[:], ev[f"tatt_ev_{c}"])
            wt("vector", ev[f"th_{c}"], ev[f"tatt_pe_{c}"])
            ev[f"gf_{c}"] = do("vector", lambda e: e.scalar_tensor_tensor(
                att_tm[:], qn_tm[:], 1.0, kn_tm[:], ALU.add, ALU.mult), "dve")

            tap("yffn", c, kn_tm[:], ev[f"ybias_{c}"])
            tap("gf", c, att_tm[:], ev[f"gf_{c}"])
            tap("cols", c, cols[:], ev[f"gf_{c}"])
            wt("tensor", ev[f"tatt_ev_{c}"], ev_cast["wo"])
            if c + 1 < nch:
                wt("tensor", ev[f"txn_ev_{c+1}"], ev[f"txn_evb_{c+1}"])
            for half in range(2):
                ev[f"wo_pe{half}_{c}"] = do("tensor", lambda e, h=half: mm_proj(
                    e, pb[2 + h][:, 0:512], kn_b, 5, h), "pe")

            wt("vector", ev[f"wo_pe1_{c}"], ev[f"gf_{c}"], ev.get(f"store_{c-2}"),
               ev[f"mix_g_{c}"])

            def fin_f(e):
                e.tensor_tensor(out_sb[p][:, 0:512], pb[2][:, 0:512],
                                att_tm[:, 0:512], ALU.mult)
                e.tensor_tensor(out_sb[p][:, 512:1024], pb[3][:, 0:512],
                                att_tm[:, 512:1024], ALU.mult)
                e.drain()
                return e.tensor_tensor(out_sb[p][:], out_sb[p][:], x_tm[p][:], ALU.add)
            ev[f"final_mul_{c}"] = do("vector", fin_f, "dve")

            wt("scalar", ev[f"final_mul_{c}"])
            ev[f"store_{c}"] = do("scalar", lambda e, c=c: e.dma_start(
                out=y_d[128 * c:128 * (c + 1), :], in_=out_sb[c & 1][:]),
                ("st0" if c % 2 == 0 else "st1", 16))

            if c + 2 < nch:
                wt("sync", ev[f"final_mul_{c}"])
                ev_ldx[c + 2] = do("sync", lambda e, c=c: e.dma_start(
                    out=x_tm[c & 1][:], in_=x_d[128 * (c + 2):128 * (c + 3), :]),
                    ("lx0" if c % 2 == 0 else "lx1", 16))

        tail(0, first=True)
        for c in range(nch):
            chunk_body(c)

        P.wait("gpsimd", ("st0", P.cnt["st0"]))
        P.wait("gpsimd", ("st1", P.cnt["st1"]))

        block = es.enter_context(nc.Block())
        P.emit(block)
    return nc


def kernel(**inputs):
    x = np.ascontiguousarray(inputs["L0_attn_x"], dtype=np.float32)
    B, T, _ = x.shape
    nch = T // LC
    wT = {
        "wq": np.ascontiguousarray(np.asarray(inputs["L0_attn_Wq"], np.float32).T),
        "wk": np.ascontiguousarray(np.asarray(inputs["L0_attn_Wk"], np.float32).T),
        "wv": np.ascontiguousarray(np.asarray(inputs["L0_attn_Wv"], np.float32).T),
        "wg": np.ascontiguousarray(np.asarray(inputs["L0_attn_Wg"], np.float32).T),
        "wf": np.ascontiguousarray(np.asarray(inputs["L0_ffn_w_int8"], np.int32).T),
        "wo": np.ascontiguousarray(np.asarray(inputs["L0_attn_Wo"], np.float32).T),
    }
    vecs = np.zeros((8, C), np.float32)
    vecs[0] = np.asarray(inputs["L0_attn_time_decay"], np.float32)
    vecs[1] = np.asarray(inputs["L0_attn_norm_scale"], np.float32)
    vecs[2] = np.asarray(inputs["L0_attn_mix_q"], np.float32).reshape(-1)
    vecs[3] = np.asarray(inputs["L0_attn_mix_k"], np.float32).reshape(-1)
    vecs[4] = np.asarray(inputs["L0_attn_mix_v"], np.float32).reshape(-1)
    vecs[5] = np.asarray(inputs["L0_attn_mix_g"], np.float32).reshape(-1)
    vecs[6] = np.asarray(inputs["L0_ffn_bias"], np.float32)
    vecs[7, 0] = np.float32(np.asarray(inputs["L0_ffn_weight_scale"]))

    nc = build_program(nch)
    in_maps = []
    for b in range(B):
        m = {"x": x[b]}
        m.update(wT)
        m["vecs"] = vecs
        in_maps.append(m)
    res = run_bass_kernel_spmd(nc, in_maps, core_ids=list(range(B)))
    if DBG_TAPS:
        import pickle
        dbg = [{k: r[k] for k in r if k.startswith("dbg_")} for r in res.results]
        with open("/tmp/ktaps.pkl", "wb") as f:
            pickle.dump(dbg, f)
    return np.stack([r["y"] for r in res.results]).astype(np.float32)


